# version 06
# baseline (speedup 1.0000x reference)
import numpy as np
import ml_dtypes
import concourse.bass as bass
import concourse.mybir as mybir
from concourse.bass_utils import run_bass_kernel_spmd
from concourse.alu_op_type import AluOpType as ALU
from contextlib import ExitStack

F32 = mybir.dt.float32
BF16 = mybir.dt.bfloat16
U8 = mybir.dt.uint8
AF = mybir.ActivationFunctionType
AX = mybir.AxisListType

C = 1024
HID = 4096
NT = 4096
ARENA_B = 80384


class Sem:
    __slots__ = ("name", "n", "handle")

    def __init__(self, name):
        self.name, self.n, self.handle = name, 0, None


class Prog:
    ENGS = ("tensor", "vector", "scalar", "sync", "gpsimd")

    def __init__(self):
        self.stream = {e: [] for e in self.ENGS}
        self.sems = []

    def sem(self, name):
        s = Sem(name)
        self.sems.append(s)
        return s

    def op(self, eng, fn, waits=(), inc=None, amt=1):
        tok = None
        if inc is not None:
            inc.n += amt
            tok = inc.n
        self.stream[eng].append(
            (fn, tuple((s, v) for s, v in waits if v), inc, amt))
        return tok

    def dma(self, eng, fn, waits=(), inc=None):
        return self.op(eng, fn, waits, inc, 16)

    def barrier(self):
        snap = [(s, s.n) for s in self.sems]
        for e in self.ENGS:
            self.stream[e].append(
                (None, tuple((s, v) for s, v in snap if v > 0), None, 0))

    def check(self):
        final = {s.name: s.n for s in self.sems}
        for e in self.ENGS:
            for fn, waits, inc, amt in self.stream[e]:
                for s, v in waits:
                    assert v <= final[s.name], \
                        f"unreachable wait {s.name}>={v} final={final[s.name]} on {e}"


def build_program(stop_after=None):
    nc = bass.Bass("TRN2", target_bir_lowering=False)
    P = Prog()

    x_in = nc.dram_tensor("x", [NT, C], F32, kind="ExternalInput").ap()
    wT_in = [nc.dram_tensor(n, [C, C], F32, kind="ExternalInput").ap()
             for n in ("wqT", "wkT", "wvT", "wpT")]
    brow_in = nc.dram_tensor("brow", [1, 4 * C], F32, kind="ExternalInput").ap()
    w_fc = nc.dram_tensor("w_fc", [HID, C], F32, kind="ExternalInput").ap()
    w_proj = nc.dram_tensor("w_proj", [C, HID], F32, kind="ExternalInput").ap()
    fb_in = nc.dram_tensor("fb", [16, 16], F32, kind="ExternalInput").ap()
    uT_in = nc.dram_tensor("uT", [16, HID], F32, kind="ExternalInput").ap()
    v_in = nc.dram_tensor("vlr", [16, C], F32, kind="ExternalInput").ap()
    gate_in = nc.dram_tensor("gate16", [16, 1], F32, kind="ExternalInput").ap()
    ident_in = nc.dram_tensor("ident", [128, 128], BF16, kind="ExternalInput").ap()
    out_hbm = nc.dram_tensor("out", [NT, C], F32, kind="ExternalOutput").ap()
    x1_hbm = nc.dram_tensor(
        "x1s", [NT, C], F32,
        kind="ExternalOutput" if stop_after in ("F1", "A") else "Internal").ap()
    x2_hbm = nc.dram_tensor(
        "x2s", [NT, C], F32,
        kind="ExternalOutput" if stop_after == "B" else "Internal").ap()
    scr_kind = "ExternalOutput" if stop_after in ("P", "A") else "Internal"
    wfc_scr = nc.dram_tensor("wfcscr", [128, 32768], BF16, kind=scr_kind).ap()
    wpr_scr = nc.dram_tensor("wprscr", [128, 32768], BF16, kind=scr_kind).ap()
    ctx_dbg = nc.dram_tensor("ctxdbg", [64, 1056], BF16,
                             kind=("ExternalOutput" if stop_after == "A"
                                   else "Internal")).ap()

    run_F1 = stop_after != "P"
    run_A = stop_after not in ("F1", "P")
    run_B = stop_after not in ("F1", "P", "A")
    run_F2 = stop_after is None

    es = ExitStack()
    with es:
        E = es.enter_context

        def replay_all():
            for s in P.sems:
                s.handle = E(nc.semaphore(s.name))
            with nc.Block() as block:
                def mk(name):
                    def body(eng):
                        seen = {}
                        for fn, waits, inc, amt in P.stream[name]:
                            for s, v in waits:
                                if seen.get(s.name, -1) < v:
                                    eng.wait_ge(s.handle, v)
                                    seen[s.name] = v
                            if fn is None:
                                continue
                            inst = fn(eng)
                            if inc is not None:
                                inst.then_inc(inc.handle, amt)
                    return body
                block.tensor(mk("tensor"))
                block.vector(mk("vector"))
                block.scalar(mk("scalar"))
                block.sync(mk("sync"))
                block.gpsimd(mk("gpsimd"))
        uber0 = E(nc.sbuf_tensor("uber0", [128, 32768], BF16))
        uber1 = E(nc.sbuf_tensor("uber1", [128, 32768], BF16))
        ident = E(nc.sbuf_tensor("ident_sb", [128, 128], BF16))
        AR = E(nc.sbuf_tensor("arena", [128, ARENA_B], U8))
        consts = E(nc.sbuf_tensor("consts", [128, 4], F32))
        banks = [E(nc.psum_tensor(f"pb{i}", [128, 512], F32)) for i in range(8)]
        sD, sG, sT, sV, sS = (P.sem(n) for n in ("sD", "sG", "sT", "sV", "sS"))

        def bankbf(i):
            return banks[i][:, 0:512].bitcast(BF16)

        def arf(off, cols, p0=0, p1=128):
            return AR[p0:p1, off:off + 4 * cols].bitcast(F32)

        def arb(off, cols, p0=0, p1=128):
            return AR[p0:p1, off:off + 2 * cols].bitcast(BF16)

        F_XG = (0, 16384)
        F_XNT = 32768
        F_XN = 40960
        F_RT = (49152, 50176)
        F_STATS = 51200
        F_R2 = 51328

        P_WL = (0, 16384)
        P_NAT = (32768, 40960)
        P_SGN = 49152
        P_ABS = 53248
        P_STT = 57344
        P_FBB = 61440
        P_GT = 61504
        P_ST = 61568
        P_XSB = 61952
        P_UTB = 70144
        P_VBF = 78336

        AB_XGA = (0, 16384)
        AB_XN = (32768, 34816)
        AB_EKQ = 36864
        AB_VAUG = 38912
        AB_RK = 41088
        AB_BC = 41216
        AB_QH = 43264
        AB_AD = 45312
        AB_ATT = 45440
        AB_ATT2 = 47488
        AB_XO = (49536, 53632)
        AB_CTX = 57728
        AB_EQT = 59904
        AB_BROW = 64000
        AB_ONES = 72192
        AB_ST = 72448

        def LN(src, dst_bf, st_off, waits_v=(), waits_s=(), waits_apply=()):
            st = arf(st_off, 8)
            t0 = P.op("vector", lambda v, src=src, st=st: v.tensor_reduce(
                out=st[:, 0:1], in_=src, axis=AX.X, op=ALU.add),
                waits=waits_v, inc=sV)
            t1 = P.op("scalar", lambda s, src=src, st=st, dst=dst_bf:
                      s.activation(out=dst, in_=src, func=AF.Square,
                                   accum_out=st[:, 1:2]),
                      waits=tuple(waits_s) + tuple(waits_apply), inc=sS)
            tm = P.op("vector", lambda v, st=st: v.tensor_scalar_mul(
                out=st[:, 2:3], in0=st[:, 0:1], scalar1=1.0 / C),
                waits=[(sV, t0)], inc=sV)
            tm2 = P.op("vector", lambda v, st=st: v.tensor_tensor(
                out=st[:, 3:4], in0=st[:, 2:3], in1=st[:, 2:3], op=ALU.mult),
                waits=[(sV, tm)], inc=sV)
            t2 = P.op("vector", lambda v, st=st: v.tensor_scalar(
                out=st[:, 4:5], in0=st[:, 1:2], scalar1=1.0 / C,
                scalar2=st[:, 3:4], op0=ALU.mult, op1=ALU.subtract),
                waits=[(sS, t1), (sV, tm2)], inc=sV)
            tln = P.op("scalar", lambda s, st=st: s.activation(
                out=st[:, 5:6], in_=st[:, 4:5], func=AF.Ln,
                bias=consts[:, 0:1]), waits=[(sV, t2)], inc=sS)
            t3 = P.op("scalar", lambda s, st=st: s.activation(
                out=st[:, 6:7], in_=st[:, 5:6], func=AF.Exp, scale=-0.5),
                waits=[(sS, tln)], inc=sS)
            return P.op("vector", lambda v, src=src, dst=dst_bf, st=st:
                        v.tensor_scalar(out=dst, in0=src, scalar1=st[:, 2:3],
                                        scalar2=st[:, 6:7], op0=ALU.subtract,
                                        op1=ALU.mult),
                        waits=list(waits_apply) + [(sS, t3)], inc=sV)

        def T8(src_bf, bank_i, waits=()):
            pb = bankbf(bank_i)
            tok = None
            for cc in range(8):
                tok = P.op("tensor", lambda t, cc=cc, src=src_bf, pb=pb:
                           t.transpose(out=pb[:, cc * 128:(cc + 1) * 128],
                                       in_=src[:, cc * 128:(cc + 1) * 128],
                                       identity=ident[:]),
                           waits=(waits if cc == 0 else ()),
                           inc=(sT if cc == 7 else None))
            return tok

        fb_bf = arb(P_FBB, 16, 0, 16)
        gt = arf(P_GT, 1, 0, 16)
        uT_bf = arb(P_UTB, HID, 0, 16)
        v_bf = arb(P_VBF, C, 0, 16)
        X_sb = arb(P_XSB, HID, 0, 16)
        pst = arf(P_ST, 32)

        P.op("vector", lambda v: v.memset(consts[:, 0:1], 1e-5))
        t_id = P.dma("sync", lambda s: s.dma_start(out=ident[:], in_=ident_in),
                     inc=sD)
        t_gt = P.dma("sync", lambda s: s.dma_start(out=gt, in_=gate_in), inc=sD)
        t_fb = P.dma("gpsimd", lambda g: g.dma_start(out=fb_bf, in_=fb_in),
                     inc=sG)
        t_ut = P.dma("gpsimd", lambda g: g.dma_start(out=uT_bf, in_=uT_in),
                     inc=sG)
        t_vl = P.dma("gpsimd", lambda g: g.dma_start(out=v_bf, in_=v_in),
                     inc=sG)

        xev = 0
        for jb in range(8):
            w = [(sG, t_fb), (sG, t_ut)] if jb == 0 else [(sV, xev)]
            tk = P.op("tensor", lambda t, jb=jb: t.matmul(
                out=banks[7][0:16, 0:512], lhsT=fb_bf,
                rhs=uT_bf[:, jb * 512:(jb + 1) * 512], start=True, stop=True),
                waits=w, inc=sT)
            xev = P.op("vector", lambda v, jb=jb: v.tensor_scalar_mul(
                out=X_sb[:, jb * 512:(jb + 1) * 512],
                in0=banks[7][0:16, 0:512], scalar1=gt),
                waits=[(sT, tk), (sD, t_gt)], inc=sV)

        wl = [arf(P_WL[i], HID) for i in range(2)]
        nat = [arb(P_NAT[i], HID) for i in range(2)]
        sgn = arf(P_SGN, C)
        ab = arf(P_ABS, C)
        sttb = arf(P_STT, C)
        sttb2 = arf(P_NAT[0] + 2048, C)

        cons = [dict(v=0, s=0) for _ in range(2)]
        evac_tok = 0
        last_stt = 0
        tmask = 0
        ttk_prev = 0
        u1v = uber1.rearrange("p (k j c) -> p k j c", k=4, j=8)
        for dt in range(8):
            t_wl = P.dma("sync", lambda s, dt=dt: s.dma_start(
                out=wl[0], in_=w_proj[dt * 128:(dt + 1) * 128, :]),
                waits=[(sV, cons[0]["v"]), (sS, cons[0]["s"])], inc=sD)
            t_abs = P.op("scalar", lambda s: s.activation(
                out=wl[1], in_=wl[0], func=AF.Abs),
                waits=[(sD, t_wl), (sV, cons[0]["v"])], inc=sS)
            t0 = P.op("vector", lambda v: v.tensor_reduce(
                out=pst[:, 0:1], in_=wl[1], axis=AX.X, op=ALU.add),
                waits=[(sS, t_abs)], inc=sV)
            tm0 = P.op("vector", lambda v: v.tensor_scalar_mul(
                out=pst[:, 1:2], in0=pst[:, 0:1], scalar1=-1.0 / HID),
                waits=[(sV, t0)], inc=sV)
            teq = 0
            for q in range(4):
                qs = slice(q * C, (q + 1) * C)
                dbuf = sgn if q % 2 == 0 else ab
                tds = P.op("scalar", lambda s, qs=qs, dbuf=dbuf: s.activation(
                    out=dbuf, in_=wl[1][:, qs],
                    func=AF.Identity, bias=pst[:, 1:2]),
                    waits=([(sV, tm0)] if q == 0 else
                           ([(sV, teq)] if q >= 2 else [])), inc=sS)
                teq = P.op("vector", lambda v, q=q, dbuf=dbuf: v.tensor_reduce(
                    out=pst[:, 3 + q:4 + q], in_=dbuf,
                    axis=AX.X, op=ALU.add), waits=[(sS, tds)], inc=sV)
            tE = P.op("vector", lambda v: v.tensor_reduce(
                out=pst[:, 7:8], in_=pst[:, 3:7], axis=AX.X, op=ALU.add),
                waits=[(sV, teq)], inc=sV)
            tEp = P.op("vector", lambda v: v.tensor_scalar_mul(
                out=pst[:, 2:3], in0=pst[:, 7:8], scalar1=1.0 / HID),
                waits=[(sV, tE)], inc=sV)
            tv1 = P.op("vector", lambda v: v.tensor_scalar(
                out=pst[:, 7:8], in0=pst[:, 2:3], scalar1=pst[:, 1:2],
                scalar2=consts[:, 0:1], op0=ALU.subtract, op1=ALU.max),
                waits=[(sV, tEp)], inc=sV)
            tsc = P.op("vector", lambda v: v.tensor_scalar_mul(
                out=pst[:, 7:8], in0=pst[:, 7:8], scalar1=0.5),
                waits=[(sV, tv1)], inc=sV)
            t_s = None
            for q in range(4):
                qs = slice(q * C, (q + 1) * C)
                td2 = P.op("scalar", lambda s, qs=qs: s.activation(
                    out=ab, in_=wl[1][:, qs], func=AF.Identity, scale=2.0,
                    bias=pst[:, 1:2]),
                    waits=([(sV, tsc)] if q == 0 else
                           [(sV, last_stt)]), inc=sS)
                tmask = P.op("vector", lambda v: v.tensor_scalar(
                    out=ab, in0=ab, scalar1=pst[:, 2:3], scalar2=1.0,
                    op0=ALU.is_gt, op1=ALU.mult),
                    waits=[(sS, td2)], inc=sV)
                t_s = P.op("scalar", lambda s, qs=qs: s.activation(
                    out=sgn, in_=wl[0][:, qs], func=AF.Sign), inc=sS)
                wst = [(sS, t_s), (sV, tmask)]
                if q == 0:
                    wst.append((sT, ttk_prev))
                last_stt = P.op("vector", lambda v, qs=qs:
                                v.scalar_tensor_tensor(
                                    out=nat[0][:, qs], in0=sgn,
                                    scalar=pst[:, 7:8], in1=ab,
                                    op0=ALU.mult, op1=ALU.mult),
                                waits=wst, inc=sV)
            cons[0] = dict(v=last_stt, s=t_s)
            ttk = None
            for jc in range(32):
                w = ([(sV, last_stt), (sS, evac_tok), (sV, xev), (sD, t_id)]
                     if jc == 0 else [])
                ttk = P.op("tensor", lambda t, jc=jc: t.transpose(
                    out=bankbf(jc % 8)[:, (jc // 8) * 128:(jc // 8 + 1) * 128],
                    in_=nat[0][:, jc * 128:(jc + 1) * 128], identity=ident[:]),
                    waits=w, inc=(sT if jc == 31 else None))
            ttk_prev = ttk
            for bb in range(8):
                evac_tok = P.op("scalar", lambda s, bb=bb, dt=dt: s.copy(
                    out=u1v[:, :, bb, dt * 128:(dt + 1) * 128],
                    in_=bankbf(bb)[:, 0:512].rearrange("p (k n) -> p k n", k=4)),
                    waits=([(sT, ttk)] if bb == 0 else []), inc=sS)
        wproj_evac_last = evac_tok

        fc_evac = 0
        lr_tok = 0
        ttk_pp = [ttk_prev, ttk_prev]
        sbufs = [sgn, ab, sttb]
        for jt0 in range(0, 32, 4):
            wsrc = w_fc[jt0 * 128:(jt0 + 4) * 128, :].rearrange(
                "(s p) c -> p s c", p=128)
            t_wl_fc = P.dma(
                "sync", lambda s, wsrc=wsrc:
                s.dma_start(out=wl[0].rearrange("p (s c) -> p s c", c=C),
                            in_=wsrc),
                waits=[(sV, cons[0]["v"]), (sS, cons[0]["s"])], inc=sD)
            t_absf = P.op("scalar", lambda s: s.activation(
                out=wl[1], in_=wl[0], func=AF.Abs),
                waits=[(sD, t_wl_fc), (sV, cons[0]["v"])], inc=sS)
            tred = None
            for q in range(4):
                tred = P.op("vector", lambda v, q=q: v.tensor_reduce(
                    out=pst[:, q:q + 1], in_=wl[1][:, q * C:(q + 1) * C],
                    axis=AX.X, op=ALU.add),
                    waits=[(sS, t_absf)] if q == 0 else [],
                    inc=(sV if q == 3 else None))
            tm = P.op("vector", lambda v: v.tensor_scalar_mul(
                out=pst[:, 4:8], in0=pst[:, 0:4], scalar1=-1.0 / C),
                waits=[(sV, tred)], inc=sV)
            tE = [0] * 4
            for q in range(4):
                tds = P.op("scalar", lambda s, q=q: s.activation(
                    out=sbufs[q % 3], in_=wl[1][:, q * C:(q + 1) * C],
                    func=AF.Identity, bias=pst[:, 4 + q:5 + q]),
                    waits=([(sV, tm)] if q == 0 else
                           ([(sV, tE[0])] if q == 3 else [])), inc=sS)
                tE[q] = P.op("vector", lambda v, q=q: v.tensor_reduce(
                    out=pst[:, 8 + q:9 + q], in_=sbufs[q % 3],
                    axis=AX.X, op=ALU.add), waits=[(sS, tds)], inc=sV)
            tEp = P.op("vector", lambda v: v.tensor_scalar_mul(
                out=pst[:, 12:16], in0=pst[:, 8:12], scalar1=1.0 / C),
                waits=[(sV, tE[3])], inc=sV)
            tsv = P.op("vector", lambda v: v.tensor_tensor(
                out=pst[:, 16:20], in0=pst[:, 12:16], in1=pst[:, 4:8],
                op=ALU.subtract), waits=[(sV, tEp)], inc=sV)
            tcl = P.op("vector", lambda v: v.tensor_scalar(
                out=pst[:, 16:20], in0=pst[:, 16:20], scalar1=1.0,
                scalar2=consts[:, 0:1], op0=ALU.mult, op1=ALU.max),
                waits=[(sV, tsv)], inc=sV)
            t_stt = 0
            for q in range(4):
                jt = jt0 + q
                wlv = wl[0][:, q * C:(q + 1) * C]
                wla = wl[1][:, q * C:(q + 1) * C]
                dbuf = sbufs[q % 2]
                gbuf = sbufs[2]
                td2 = P.op("scalar", lambda s, wla=wla, dbuf=dbuf, q=q:
                           s.activation(out=dbuf, in_=wla, func=AF.Identity,
                                        scale=2.0, bias=pst[:, 4 + q:5 + q]),
                           waits=([(sV, tcl)] if q == 0 else
                                  [(sV, t_stt)]), inc=sS)
                t_sg = P.op("scalar", lambda s, wlv=wlv, gbuf=gbuf:
                            s.activation(out=gbuf, in_=wlv, func=AF.Sign),
                            waits=[(sV, t_stt)] if q > 0 else [], inc=sS)
                tmask = P.op("vector", lambda v, dbuf=dbuf, q=q:
                             v.tensor_scalar(
                                 out=dbuf, in0=dbuf,
                                 scalar1=pst[:, 12 + q:13 + q], scalar2=1.0,
                                 op0=ALU.is_gt, op1=ALU.mult),
                             waits=[(sS, td2)], inc=sV)
                wstt = [(sS, t_sg), (sV, tmask)]
                if jt == 0:
                    wstt.append((sT, ttk_prev))
                t_stt = P.op("vector", lambda v, dbuf=dbuf, gbuf=gbuf, q=q:
                             v.scalar_tensor_tensor(
                                 out=sttb2, in0=gbuf,
                                 scalar=pst[:, 16 + q:17 + q], in1=dbuf,
                                 op0=ALU.mult, op1=ALU.mult),
                             waits=wstt, inc=sV)
                cons[0] = dict(v=t_stt, s=t_sg)
                w0 = ([(sV, xev), (sG, t_vl), (sS, wproj_evac_last)]
                      if jt == 0 else [(sV, lr_tok)])
                P.op("tensor", lambda t, jt=jt: t.matmul(
                    out=banks[6][:, 0:512],
                    lhsT=X_sb[:, jt * 128:(jt + 1) * 128],
                    rhs=v_bf[:, 0:512], start=True, stop=True), waits=w0)
                tlr = P.op("tensor", lambda t, jt=jt: t.matmul(
                    out=banks[7][:, 0:512],
                    lhsT=X_sb[:, jt * 128:(jt + 1) * 128],
                    rhs=v_bf[:, 512:1024], start=True, stop=True), inc=sT)
                natv = nat[jt % 2][:, 0:C]
                P.op("vector", lambda v, natv=natv: v.tensor_tensor(
                    out=natv[:, 0:512], in0=sttb2[:, 0:512],
                    in1=banks[6][:, 0:512], op=ALU.add),
                    waits=[(sV, t_stt), (sT, tlr), (sT, ttk_pp[jt % 2])])
                lr_tok = P.op("vector", lambda v, natv=natv: v.tensor_tensor(
                    out=natv[:, 512:1024], in0=sttb2[:, 512:1024],
                    in1=banks[7][:, 0:512], op=ALU.add), inc=sV)
                ttk = None
                for cc in range(8):
                    if cc < 6:
                        dst = bankbf(cc)[:, (jt % 4) * 128:(jt % 4 + 1) * 128]
                    else:
                        dst = bankbf(cc - 6)[:, 512 + (jt % 4) * 128:
                                             512 + (jt % 4 + 1) * 128]
                    w = []
                    if cc == 0:
                        w = [(sV, lr_tok)]
                        if jt == 0:
                            w.append((sS, wproj_evac_last))
                        if jt % 4 == 0:
                            w.append((sS, fc_evac))
                    ttk = P.op("tensor", lambda t, dst=dst, cc=cc, natv=natv:
                               t.transpose(out=dst,
                                           in_=natv[:, cc * 128:(cc + 1) * 128],
                                           identity=ident[:]),
                               waits=w, inc=(sT if cc == 7 else None))
                ttk_pp[jt % 2] = ttk
            for cc in range(8):
                src2 = (bankbf(cc)[:, 0:512] if cc < 6
                        else bankbf(cc - 6)[:, 512:1024])
                fc_evac = P.op("scalar",
                               lambda s, cc=cc, jt0=jt0, src2=src2:
                               s.copy(out=uber0[:, cc * 4096 + jt0 * 128:
                                                cc * 4096 + jt0 * 128 + 512],
                                      in_=src2),
                               waits=([(sT, ttk)] if cc == 0 else []),
                               inc=sS)
        P.dma("gpsimd", lambda g: g.dma_start(out=wfc_scr, in_=uber0[:]),
              waits=[(sS, fc_evac)], inc=sG)
        P.dma("gpsimd", lambda g: g.dma_start(out=wpr_scr, in_=uber1[:]),
              waits=[(sS, wproj_evac_last)], inc=sG)

        P.barrier()

        early_reload = {}

        def phase_F(xin, xout, reload):
            xg = [arf(F_XG[i], HID) for i in range(2)]
            xn = [arb(F_XN + s4 * 2048, C) for s4 in range(4)]
            xntv = AR[:, F_XNT:F_XNT + 2 * HID].bitcast(BF16).rearrange(
                "p (cc n) -> p cc n", n=512)
            rt = [arb(F_RT[i], 512) for i in range(2)]
            r2 = arb(F_R2, 8192)
            re0w = re1w = (sD, 0)
            if reload:
                if early_reload:
                    re0w = early_reload["re0"]
                    re1w = early_reload["re1"]
                else:
                    re0w = (sD, P.dma("sync", lambda s: s.dma_start(
                        out=uber0[:], in_=wfc_scr), inc=sD))
                    re1w = (sD, P.dma("sync", lambda s: s.dma_start(
                        out=uber1[:], in_=wpr_scr), inc=sD))
            store_tok = [0] * 8
            t_xl = [0] * 8
            ln_tok = {}
            t8_toks = {}
            mm1_last = 0
            mm2_last = 0
            ev2_last = 0
            ev2_tok = [0, 0]
            relu_tok = [0] * 32
            sq_tok = [0] * 32
            xnt_evac = [0, 0, 0, 0]

            def emit_load(g):
                src = xin[g * 512:(g + 1) * 512, :].rearrange(
                    "(s p) c -> p s c", p=128)
                t_xl[g] = P.dma("sync", lambda s, src=src, g=g: s.dma_start(
                    out=xg[g % 2].rearrange("p (s c) -> p s c", c=C), in_=src),
                    waits=([(sD, store_tok[g - 2])] if g >= 2 else []),
                    inc=sD)

            def emit_LN_group(g):
                for s4 in range(4):
                    first = [(sD, t_xl[g])] if s4 == 0 else []
                    ln_tok[(g, s4)] = LN(
                        xg[g % 2][:, s4 * C:(s4 + 1) * C], xn[s4], F_STATS,
                        waits_v=first, waits_s=first,
                        waits_apply=[(sT, t8_toks.get((g - 1, s4), 0))])

            emit_load(0)
            emit_load(1)
            emit_LN_group(0)
            for g in range(8):
                bg = g % 2
                for s4 in range(4):
                    w = [(sV, ln_tok[(g, s4)])]
                    if s4 >= 2:
                        w.append((sS, xnt_evac[s4 - 2]))
                    elif g > 0:
                        w.append((sS, xnt_evac[s4 + 2]))
                    t8_toks[(g, s4)] = T8(xn[s4], 6 + s4 % 2, w)
                    ew = [(sT, t8_toks[(g, s4)])]
                    if s4 == 0:
                        ew.append((sT, mm1_last))
                    xnt_evac[s4] = P.op(
                        "scalar", lambda s, s4=s4: s.copy(
                            out=xntv[:, :, s4 * 128:(s4 + 1) * 128],
                            in_=bankbf(6 + s4 % 2).rearrange(
                                "p (cc n) -> p cc n", n=128)),
                        waits=ew, inc=sS)
                for half in range(2):
                    for jc in range(half * 16, half * 16 + 16):
                        tok = None
                        for cc in range(8):
                            w = []
                            if cc == 0:
                                if jc < 2:
                                    w.append((sS, xnt_evac[3]))
                                else:
                                    w.append((sS, relu_tok[jc - 2]))
                                if g == 0 and jc == 0 and reload:
                                    w.append(re0w)
                            tok = P.op(
                                "tensor", lambda t, jc=jc, cc=cc: t.matmul(
                                    out=banks[jc % 2][:, 0:512],
                                    lhsT=uber0[:, cc * 4096 + jc * 128:
                                               cc * 4096 + (jc + 1) * 128],
                                    rhs=xntv[:, cc, :],
                                    start=(cc == 0), stop=(cc == 7)),
                                waits=w, inc=(sT if cc == 7 else None))
                        mm1_last = tok
                        relu_tok[jc] = P.op(
                            "scalar", lambda s, jc=jc: s.activation(
                                out=rt[jc % 2][:], in_=banks[jc % 2][:, 0:512],
                                func=AF.Relu),
                            waits=[(sT, tok)] + (
                                [(sV, sq_tok[jc - 2])] if jc >= 2 else []),
                            inc=sS)
                        wsq = [(sS, relu_tok[jc])]
                        if jc % 16 == 0:
                            wsq.append((sT, mm2_last))
                        sq_tok[jc] = P.op(
                            "vector", lambda v, jc=jc: v.tensor_tensor(
                                out=r2[:, (jc % 16) * 512:
                                       (jc % 16 + 1) * 512],
                                in0=rt[jc % 2][:], in1=rt[jc % 2][:],
                                op=ALU.mult),
                            waits=wsq, inc=sV)
                    if half == 0 and g + 1 < 8:
                        emit_LN_group(g + 1)
                    for era in range(4):
                        dh, nsp = era // 2, era % 2
                        bp = 2 + 2 * (era % 2)
                        tok = None
                        for jc in range(half * 16, half * 16 + 16):
                            for ns2 in range(2):
                                ns = 2 * nsp + ns2
                                w = []
                                if ns2 == 0:
                                    w.append((sV, sq_tok[jc]))
                                    if jc % 16 == 0:
                                        w.append((sV, ev2_tok[era % 2]))
                                        if g == 0 and half == 0 and era == 0 \
                                                and reload:
                                            w.append(re1w)
                                tok = P.op(
                                    "tensor",
                                    lambda t, dh=dh, jc=jc, ns=ns, bp=bp,
                                    ns2=ns2: t.matmul(
                                        out=banks[bp + ns2][:, 0:512],
                                        lhsT=r2[:, (jc % 16) * 512 + ns * 128:
                                                (jc % 16) * 512 + (ns + 1) * 128],
                                        rhs=uber1[:, jc * 1024 + dh * 512:
                                                  jc * 1024 + (dh + 1) * 512],
                                        start=(jc % 16 == 0),
                                        stop=(jc % 16 == 15)),
                                    waits=w,
                                    inc=(sT if (jc % 16 == 15 and ns2 == 1)
                                         else None))
                        mm2_last = tok
                        for ns2 in range(2):
                            ns = 2 * nsp + ns2
                            ev2_tok[era % 2] = P.op(
                                "vector", lambda v, dh=dh, ns=ns, bp=bp,
                                ns2=ns2, bg=bg: v.tensor_tensor(
                                    out=xg[bg][:, ns * 1024 + dh * 512:
                                               ns * 1024 + (dh + 1) * 512],
                                    in0=xg[bg][:, ns * 1024 + dh * 512:
                                               ns * 1024 + (dh + 1) * 512],
                                    in1=banks[bp + ns2][:, 0:512], op=ALU.add),
                                waits=([(sT, mm2_last)] if ns2 == 0 else []),
                                inc=sV)
                        ev2_last = ev2_tok[era % 2]
                dst = xout[g * 512:(g + 1) * 512, :].rearrange(
                    "(s p) c -> p s c", p=128)
                store_tok[g] = P.dma(
                    "sync", lambda s, dst=dst, bg=bg: s.dma_start(
                        out=dst,
                        in_=xg[bg].rearrange("p (s c) -> p s c", c=C)),
                    waits=[(sV, ev2_last)], inc=sD)
                if g + 2 < 8:
                    emit_load(g + 2)
            return store_tok

        if run_F1:
            phase_F(x_in, x1_hbm, reload=False)
        P.barrier()

        if run_A:
            xga = [arf(AB_XGA[i], HID) for i in range(2)]
            xnab = [arb(AB_XN[i], C) for i in range(2)]
            ekq = arb(AB_EKQ, C)
            ekq3 = AR[:, AB_EKQ:AB_EKQ + 2 * C].bitcast(BF16).rearrange(
                "p (h e) -> p h e", e=64)
            vaug = arb(AB_VAUG, 1056)
            vaug3 = AR[:, AB_VAUG:AB_VAUG + 2 * 1056].bitcast(BF16).rearrange(
                "p (h e) -> p h e", e=66)
            rk = arf(AB_RK, 32)
            bc = arb(AB_BC, C)
            bc3 = AR[:, AB_BC:AB_BC + 2 * C].bitcast(BF16).rearrange(
                "p (h e) -> p h e", e=64)
            qh = arb(AB_QH, C)
            aD = arf(AB_AD, 16)
            att = arb(AB_ATT, C)
            att3 = AR[:, AB_ATT:AB_ATT + 2 * C].bitcast(BF16).rearrange(
                "p (h e) -> p h e", e=64)
            attT = arb(AB_ATT2, C)
            xo = [arf(AB_XO[i], C) for i in range(2)]
            ctx = arb(AB_CTX, 1056, 0, 64)
            eqT = arb(AB_EQT, 2048, 0, 64)
            eqT3 = AR[0:64, AB_EQT:AB_EQT + 2 * 2048].bitcast(BF16).rearrange(
                "p (h2 two n) -> p h2 two n", two=2, n=128)
            brow = arb(AB_BROW, 4 * C, 0, 1)
            ones = arb(AB_ONES, 128, 0, 1)
            u0v = uber0.rearrange("p (cc n) -> p cc n", n=4096)

            t_ones = P.op("vector", lambda v: v.memset(ones, 1.0), inc=sV)
            t_brow = P.dma("gpsimd", lambda g: g.dma_start(
                out=brow, in_=brow_in), inc=sG)
            t_w = [0] * 4
            for widx in (1, 2, 0, 3):
                for cc in range(8):
                    t_w[widx] = P.dma(
                        "gpsimd", lambda g, widx=widx, cc=cc: g.dma_start(
                            out=uber1[:, widx * 8192 + cc * 1024:
                                      widx * 8192 + (cc + 1) * 1024],
                            in_=wT_in[widx][cc * 128:(cc + 1) * 128, :]),
                        inc=sG)

            def emit_proj(widx, bank0, tc, extra_w=()):
                tok = None
                for jh in range(2):
                    for cc in range(8):
                        w = list(extra_w) if (jh == 0 and cc == 0) else []
                        P.op("tensor", lambda t, widx=widx, jh=jh, cc=cc,
                             tc=tc, bank0=bank0: t.matmul(
                                 out=banks[bank0 + jh][:, 0:512],
                                 lhsT=u0v[:, cc, tc * 128:(tc + 1) * 128],
                                 rhs=uber1[:,
                                           widx * 8192 + cc * 1024 + jh * 512:
                                           widx * 8192 + cc * 1024
                                           + (jh + 1) * 512],
                                 start=(cc == 0), stop=False), waits=w)
                    tok = P.op("tensor",
                               lambda t, widx=widx, jh=jh, bank0=bank0:
                               t.matmul(out=banks[bank0 + jh][:, 0:512],
                                        lhsT=ones,
                                        rhs=brow[0:1, widx * C + jh * 512:
                                                 widx * C + (jh + 1) * 512],
                                        start=False, stop=True), inc=sT)
                return tok

            ln_a = [0] * 32
            t_xga = [0] * 8
            t8a = [0] * 32
            xnt_ev_a = [0] * 32
            kmm_tok = [0] * 32
            ek_tok = 0
            ctx_tok = 0
            vaug_tok = 0

            def emit_load_A(ga, cons_tok, toks):
                src = x1_hbm[ga * 512:(ga + 1) * 512, :].rearrange(
                    "(s p) c -> p s c", p=128)
                toks[ga] = P.dma("sync", lambda s, src=src, ga=ga: s.dma_start(
                    out=xga[ga % 2].rearrange("p (s c) -> p s c", c=C),
                    in_=src),
                    waits=([(sV, cons_tok)] if cons_tok else []), inc=sD)

            def emit_LN_A(tc, t8prev):
                first = [(sD, t_xga[tc // 4])] if tc % 4 == 0 else []
                ln_a[tc] = LN(
                    xga[(tc // 4) % 2][:, (tc % 4) * C:(tc % 4 + 1) * C],
                    xnab[tc % 2], AB_ST,
                    waits_v=first, waits_s=first,
                    waits_apply=[(sT, t8prev)])

            emit_load_A(0, 0, t_xga)
            emit_LN_A(0, 0)
            t8a[0] = T8(xnab[0], 7, [(sV, ln_a[0])])
            xnt_ev_a[0] = P.op("scalar", lambda s: s.copy(
                out=u0v[:, :, 0:128],
                in_=bankbf(7).rearrange("p (cc n) -> p cc n", n=128)),
                waits=[(sT, t8a[0])], inc=sS)
            kmm_tok[0] = emit_proj(1, 0, 0,
                                   extra_w=[(sG, t_w[1]), (sV, t_ones),
                                            (sG, t_brow), (sS, xnt_ev_a[0])])
            for tc in range(32):
                if tc + 1 < 32:
                    if (tc + 1) % 4 == 0:
                        ga = (tc + 1) // 4
                        emit_load_A(ga,
                                    ln_a[4 * (ga - 2) + 3] if ga >= 2 else 0,
                                    t_xga)
                    emit_LN_A(tc + 1, t8a[tc])
                w = [(sT, kmm_tok[tc])]
                if tc >= 1:
                    w.append((sT, ctx_tok))
                for jh in range(2):
                    ek_tok = P.op("scalar", lambda s, jh=jh: s.activation(
                        out=ekq[:, jh * 512:(jh + 1) * 512],
                        in_=banks[jh][:, 0:512], func=AF.Exp),
                        waits=(w if jh == 0 else []), inc=sS)
                tsk = P.op("vector", lambda v: v.tensor_reduce(
                    out=rk[:, 0:16], in_=ekq3, axis=AX.X, op=ALU.add),
                    waits=[(sS, ek_tok)], inc=sV)
                trk = P.op("vector", lambda v: v.reciprocal(
                    out=rk[:, 16:32], in_=rk[:, 0:16]),
                    waits=[(sV, tsk)], inc=sV)
                tbc = P.op("vector", lambda v: v.tensor_copy(
                    out=bc3, in_=rk[:, 16:32].unsqueeze(2).broadcast_to(
                        [128, 16, 64])), waits=[(sV, trk)], inc=sV)
                vmm = emit_proj(2, 2, tc,
                                extra_w=([(sG, t_w[2])] if tc == 0 else
                                         [(sV, vaug_tok)]))
                if tc + 1 < 32:
                    t8a[tc + 1] = T8(xnab[(tc + 1) % 2], 7,
                                     [(sV, ln_a[tc + 1]), (sS, xnt_ev_a[tc])])
                    xnt_ev_a[tc + 1] = P.op(
                        "scalar", lambda s, tc=tc: s.copy(
                            out=u0v[:, :, (tc + 1) * 128:(tc + 2) * 128],
                            in_=bankbf(7).rearrange("p (cc n) -> p cc n",
                                                    n=128)),
                        waits=[(sT, t8a[tc + 1])], inc=sS)
                    kmm_tok[tc + 1] = emit_proj(
                        1, 0, tc + 1,
                        extra_w=[(sS, xnt_ev_a[tc + 1]), (sS, ek_tok)])
                w = [(sT, vmm), (sV, tbc)]
                if tc >= 1:
                    w.append((sT, ctx_tok))
                for b2 in range(2):
                    P.op("vector", lambda v, b2=b2: v.tensor_tensor(
                        out=vaug3[:, b2 * 8:(b2 + 1) * 8, 0:64],
                        in0=banks[2 + b2].rearrange("p (h e) -> p h e", e=64),
                        in1=bc3[:, b2 * 8:(b2 + 1) * 8, :], op=ALU.mult),
                        waits=(w if b2 == 0 else []))
                vaug_tok = P.op("vector", lambda v: v.tensor_copy(
                    out=vaug3[:, :, 64:65],
                    in_=rk[:, 16:32].unsqueeze(2)), inc=sV)
                for h in range(16):
                    w = [(sV, vaug_tok)] if h == 0 else []
                    ctx_tok = P.op("tensor", lambda t, h=h, tc=tc: t.matmul(
                        out=banks[4 + h // 7][0:64,
                                              (h % 7) * 66:(h % 7) * 66 + 65],
                        lhsT=ekq[:, h * 64:(h + 1) * 64],
                        rhs=vaug[:, h * 66:h * 66 + 65],
                        start=(tc == 0 and h % 7 == 0),
                        stop=(tc == 31 and (h % 7 == 6 or h == 15))),
                        waits=w, inc=(sT if h == 15 else None))
            ctx_ev = 0
            for b in range(3):
                nh = (7, 7, 2)[b]
                ctx_ev = P.op("scalar", lambda s, b=b, nh=nh: s.copy(
                    out=ctx[:, b * 462:b * 462 + nh * 66],
                    in_=banks[4 + b][0:64, 0:nh * 66]),
                    waits=([(sT, ctx_tok)] if b == 0 else []), inc=sS)
            if stop_after == "A":
                P.dma("sync", lambda s: s.dma_start(out=ctx_dbg, in_=ctx),
                      waits=[(sS, ctx_ev)], inc=sD)
                P.dma("sync", lambda s: s.dma_start(out=wfc_scr, in_=uber0[:]),
                      waits=[(sS, ctx_ev)], inc=sD)

            P.barrier()

            if not run_B:
                P.barrier()
                P.check()
                replay_all()
                return nc
            eq_tok = 0
            qh_tok = 0
            att1_last = 0
            attT_ev = 0
            o1_tok = 0
            x2ev = [0] * 32
            x2st = [0] * 32
            qmm_tok = [0] * 32
            t_xgb = [0] * 8

            def emit_load_B(ga):
                src = x1_hbm[ga * 512:(ga + 1) * 512, :].rearrange(
                    "(s p) c -> p s c", p=128)
                t_xgb[ga] = P.dma("sync", lambda s, src=src, ga=ga: s.dma_start(
                    out=xga[ga % 2].rearrange("p (s c) -> p s c", c=C),
                    in_=src),
                    waits=([(sV, x2ev[4 * (ga - 2) + 3])] if ga >= 2 else []),
                    inc=sD)

            emit_load_B(0)
            qmm_tok[0] = emit_proj(0, 0, 0, extra_w=[(sG, t_w[0])])
            t8q_prev = 0
            t8t_prev = 0
            attT_evs = [0] * 32
            proj_tok = [0] * 32

            def emit_projB(tc):
                pw = [(sS, attT_evs[tc])]
                if tc == 0:
                    pw.append((sG, t_w[3]))
                if tc >= 1:
                    pw.append((sV, x2ev[tc - 1]))
                ptok = None
                for jh in range(2):
                    for cc in range(8):
                        w = pw if (jh == 0 and cc == 0) else []
                        P.op("tensor", lambda t, jh=jh, cc=cc: t.matmul(
                            out=banks[5 + jh][:, 0:512],
                            lhsT=attT[:, cc * 128:(cc + 1) * 128],
                            rhs=uber1[:, 3 * 8192 + cc * 1024 + jh * 512:
                                      3 * 8192 + cc * 1024 + (jh + 1) * 512],
                            start=(cc == 0), stop=False), waits=w)
                    ptok = P.op("tensor", lambda t, jh=jh: t.matmul(
                        out=banks[5 + jh][:, 0:512], lhsT=ones,
                        rhs=brow[0:1, 3 * C + jh * 512:3 * C + (jh + 1) * 512],
                        start=False, stop=True), inc=sT)
                proj_tok[tc] = ptok

            def emit_x2(tc):
                for jh in range(2):
                    w = []
                    if jh == 0:
                        w = [(sT, proj_tok[tc]), (sD, t_xgb[tc // 4])]
                        if tc >= 2:
                            w.append((sD, x2st[tc - 2]))
                    x2ev[tc] = P.op(
                        "vector", lambda v, jh=jh, tc=tc: v.tensor_tensor(
                            out=xo[tc % 2][:, jh * 512:(jh + 1) * 512],
                            in0=banks[5 + jh][:, 0:512],
                            in1=xga[(tc // 4) % 2][:,
                                                   (tc % 4) * 1024 + jh * 512:
                                                   (tc % 4) * 1024
                                                   + (jh + 1) * 512],
                            op=ALU.add), waits=w, inc=sV)
                x2st[tc] = P.dma("sync", lambda s, tc=tc: s.dma_start(
                    out=x2_hbm[tc * 128:(tc + 1) * 128, :], in_=xo[tc % 2]),
                    waits=[(sV, x2ev[tc])], inc=sD)

            for tc in range(32):
                if (tc + 1) % 4 == 0 and tc + 1 < 32:
                    emit_load_B((tc + 1) // 4)
                w = [(sT, qmm_tok[tc])]
                if tc >= 1:
                    w.append((sV, qh_tok))
                for jh in range(2):
                    eq_tok = P.op("scalar", lambda s, jh=jh: s.activation(
                        out=ekq[:, jh * 512:(jh + 1) * 512],
                        in_=banks[jh][:, 0:512], func=AF.Exp),
                        waits=(w if jh == 0 else []), inc=sS)
                tsq = P.op("vector", lambda v: v.tensor_reduce(
                    out=rk[:, 0:16], in_=ekq3, axis=AX.X, op=ALU.add),
                    waits=[(sS, eq_tok)], inc=sV)
                trq = P.op("vector", lambda v: v.reciprocal(
                    out=rk[:, 16:32], in_=rk[:, 0:16]),
                    waits=[(sV, tsq)], inc=sV)
                tbc = P.op("vector", lambda v: v.tensor_copy(
                    out=bc3, in_=rk[:, 16:32].unsqueeze(2).broadcast_to(
                        [128, 16, 64])), waits=[(sV, trq)], inc=sV)
                qh_tok = P.op("vector", lambda v: v.tensor_tensor(
                    out=qh, in0=ekq, in1=bc, op=ALU.mult),
                    waits=[(sV, tbc)], inc=sV)
                if tc + 1 < 32:
                    qmm_tok[tc + 1] = emit_proj(0, 0, tc + 1,
                                                extra_w=[(sS, eq_tok)])
                w = [(sV, qh_tok)]
                if tc >= 1:
                    w.append((sS, attT_evs[tc - 1]))
                t8q = T8(qh, 7, w)
                t8q_prev = t8q
                pb7 = bankbf(7)
                ew = [(sT, t8q)]
                if tc >= 1:
                    ew.append((sT, o1_tok))
                P.op("scalar", lambda s: s.copy(
                    out=eqT3[:, :, 0, :],
                    in_=pb7[0:64, :].rearrange("p (cc n) -> p cc n", n=128)),
                    waits=ew)
                eqT_ev = P.op("scalar", lambda s: s.copy(
                    out=eqT3[:, :, 1, :],
                    in_=pb7[64:128, :].rearrange("p (cc n) -> p cc n", n=128)),
                    inc=sS)
                for h in range(16):
                    w = []
                    if h == 0:
                        w = [(sS, eqT_ev), (sS, ctx_ev)]
                        if tc >= 1:
                            w.append((sV, att1_last))
                    o1_tok = P.op("tensor", lambda t, h=h: t.matmul(
                        out=banks[2 + h // 7][:,
                                              (h % 7) * 66:(h % 7) * 66 + 65],
                        lhsT=eqT[:, h * 128:(h + 1) * 128],
                        rhs=ctx[:, h * 66:h * 66 + 65],
                        start=(h % 7 == 0),
                        stop=(h % 7 == 6 or h == 15)),
                        waits=w, inc=(sT if h == 15 else None))
                if tc >= 1:
                    emit_projB(tc - 1)
                tad = None
                for b in range(3):
                    nh = (7, 7, 2)[b]
                    tad = P.op("vector", lambda v, b=b, nh=nh: v.reciprocal(
                        out=aD[:, b * 7:b * 7 + nh],
                        in_=banks[2 + b][:, 0:nh * 66].rearrange(
                            "p (h e) -> p h e", e=66)[:, :, 64:65]),
                        waits=([(sT, o1_tok)] if b == 0 else []),
                        inc=(sV if b == 2 else None))
                tadb = P.op("vector", lambda v: v.tensor_copy(
                    out=bc3, in_=aD[:, 0:16].unsqueeze(2).broadcast_to(
                        [128, 16, 64])), waits=[(sV, tad)], inc=sV)
                for b in range(3):
                    nh = (7, 7, 2)[b]
                    w = [(sV, tadb)] if b == 0 else []
                    if b == 0 and tc >= 1:
                        w.append((sT, t8t_prev))
                    P.op("vector", lambda v, b=b, nh=nh: v.tensor_tensor(
                        out=att3[:, b * 7:b * 7 + nh, :],
                        in0=banks[2 + b][:, 0:nh * 66].rearrange(
                            "p (h e) -> p h e", e=66)[:, :, 0:64],
                        in1=bc3[:, b * 7:b * 7 + nh, :], op=ALU.mult),
                        waits=w)
                att1_last = P.op("vector", lambda v: v.tensor_tensor(
                    out=att, in0=att, in1=qh, op=ALU.add), inc=sV)
                t8t = T8(att, 7, [(sV, att1_last), (sS, eqT_ev)])
                t8t_prev = t8t
                aw = [(sT, t8t)]
                if tc >= 1:
                    aw.append((sT, proj_tok[tc - 1]))
                attT_evs[tc] = P.op("scalar", lambda s: s.copy(
                    out=attT, in_=bankbf(7)[:, 0:1024]),
                    waits=aw, inc=sS)
                if tc >= 1:
                    emit_x2(tc - 1)
            emit_projB(31)
            emit_x2(31)
            if run_F2:
                early_reload["re0"] = (sD, P.dma(
                    "sync", lambda s: s.dma_start(out=uber0[:], in_=wfc_scr),
                    waits=[(sT, qmm_tok[31])], inc=sD))
                early_reload["re1"] = (sG, P.dma(
                    "gpsimd", lambda g: g.dma_start(out=uber1[:], in_=wpr_scr),
                    waits=[(sT, proj_tok[31])], inc=sG))

            P.barrier()

        if run_F2:
            phase_F(x2_hbm, out_hbm, reload=True)

        P.barrier()
        P.check()

        replay_all()
    return nc


def kernel(_stop_after=None, **inputs):
    f32 = np.float32
    x = np.asarray(inputs["L0_attn_x"], dtype=f32)

    def T(a):
        return np.ascontiguousarray(np.asarray(a, f32).T)

    shared = {
        "wqT": T(inputs["L0_attn_Wq"]),
        "wkT": T(inputs["L0_attn_Wk"]),
        "wvT": T(inputs["L0_attn_Wv"]),
        "wpT": T(inputs["L0_attn_Wp"]),
        "brow": np.ascontiguousarray(np.stack(
            [np.asarray(inputs["L0_attn_b" + k], f32) for k in "qkvp"]
        ).reshape(1, 4 * C)),
        "w_fc": np.ascontiguousarray(np.asarray(inputs["L0_ffn_w_fc"], f32)),
        "w_proj": np.ascontiguousarray(np.asarray(inputs["L0_ffn_w_proj"],
                                                  f32)),
        "fb": np.ascontiguousarray(np.asarray(inputs["L0_ffn_fast_b"], f32)),
        "uT": T(inputs["L0_ffn_u"]),
        "vlr": np.ascontiguousarray(np.asarray(inputs["L0_ffn_v"], f32)),
        "gate16": np.ascontiguousarray(np.broadcast_to(
            np.asarray(inputs["L0_ffn_gate"], f32).reshape(1, 1), (16, 1))),
        "ident": np.eye(128, dtype=ml_dtypes.bfloat16),
    }
    nc = build_program(_stop_after)
    in_maps = [dict(shared, x=np.ascontiguousarray(x[i])) for i in range(8)]
    res = run_bass_kernel_spmd(nc, in_maps, core_ids=list(range(8)))
    if _stop_after == "P":
        return (np.asarray(res.results[0]["wfcscr"]),
                np.asarray(res.results[0]["wprscr"]))
    if _stop_after == "A":
        return (np.asarray(res.results[0]["ctxdbg"]),
                np.asarray(res.results[0]["wfcscr"]),
                np.stack([np.asarray(r["x1s"]) for r in res.results]))
    key = {"F1": "x1s", "B": "x2s"}.get(_stop_after, "out")
    return np.stack([np.asarray(r[key]) for r in res.results])


# version 07
# speedup vs baseline: 1.0205x; 1.0205x over previous
import numpy as np
import ml_dtypes
import concourse.bass as bass
import concourse.mybir as mybir
from concourse.bass_utils import run_bass_kernel_spmd
from concourse.alu_op_type import AluOpType as ALU
from contextlib import ExitStack

F32 = mybir.dt.float32
BF16 = mybir.dt.bfloat16
U8 = mybir.dt.uint8
AF = mybir.ActivationFunctionType
AX = mybir.AxisListType

C = 1024
HID = 4096
NT = 4096
ARENA_B = 80384


class Sem:
    __slots__ = ("name", "n", "handle")

    def __init__(self, name):
        self.name, self.n, self.handle = name, 0, None


class Prog:
    ENGS = ("tensor", "vector", "scalar", "sync", "gpsimd")

    def __init__(self):
        self.stream = {e: [] for e in self.ENGS}
        self.sems = []

    def sem(self, name):
        s = Sem(name)
        self.sems.append(s)
        return s

    def op(self, eng, fn, waits=(), inc=None, amt=1):
        tok = None
        if inc is not None:
            inc.n += amt
            tok = inc.n
        self.stream[eng].append(
            (fn, tuple((s, v) for s, v in waits if v), inc, amt))
        return tok

    def dma(self, eng, fn, waits=(), inc=None):
        return self.op(eng, fn, waits, inc, 16)

    def barrier(self):
        snap = [(s, s.n) for s in self.sems]
        for e in self.ENGS:
            self.stream[e].append(
                (None, tuple((s, v) for s, v in snap if v > 0), None, 0))

    def check(self):
        final = {s.name: s.n for s in self.sems}
        for e in self.ENGS:
            for fn, waits, inc, amt in self.stream[e]:
                for s, v in waits:
                    assert v <= final[s.name], \
                        f"unreachable wait {s.name}>={v} final={final[s.name]} on {e}"


def build_program(stop_after=None):
    nc = bass.Bass("TRN2", target_bir_lowering=False)
    P = Prog()

    x_in = nc.dram_tensor("x", [NT, C], F32, kind="ExternalInput").ap()
    wT_in = [nc.dram_tensor(n, [C, C], F32, kind="ExternalInput").ap()
             for n in ("wqT", "wkT", "wvT", "wpT")]
    brow_in = nc.dram_tensor("brow", [1, 4 * C], F32, kind="ExternalInput").ap()
    w_fc = nc.dram_tensor("w_fc", [HID, C], F32, kind="ExternalInput").ap()
    w_proj = nc.dram_tensor("w_proj", [C, HID], F32, kind="ExternalInput").ap()
    fb_in = nc.dram_tensor("fb", [16, 16], F32, kind="ExternalInput").ap()
    uT_in = nc.dram_tensor("uT", [16, HID], F32, kind="ExternalInput").ap()
    v_in = nc.dram_tensor("vlr", [16, C], F32, kind="ExternalInput").ap()
    gate_in = nc.dram_tensor("gate16", [16, 1], F32, kind="ExternalInput").ap()
    ident_in = nc.dram_tensor("ident", [128, 128], BF16, kind="ExternalInput").ap()
    out_hbm = nc.dram_tensor("out", [NT, C], F32, kind="ExternalOutput").ap()
    x1_hbm = nc.dram_tensor(
        "x1s", [NT, C], F32,
        kind="ExternalOutput" if stop_after in ("F1", "A") else "Internal").ap()
    x2_hbm = nc.dram_tensor(
        "x2s", [NT, C], F32,
        kind="ExternalOutput" if stop_after == "B" else "Internal").ap()
    scr_kind = "ExternalOutput" if stop_after in ("P", "A") else "Internal"
    wfc_scr = nc.dram_tensor("wfcscr", [128, 32768], BF16, kind=scr_kind).ap()
    wpr_scr = nc.dram_tensor("wprscr", [128, 32768], BF16, kind=scr_kind).ap()
    ctx_dbg = nc.dram_tensor("ctxdbg", [64, 1056], BF16,
                             kind=("ExternalOutput" if stop_after == "A"
                                   else "Internal")).ap()

    run_F1 = stop_after != "P"
    run_A = stop_after not in ("F1", "P")
    run_B = stop_after not in ("F1", "P", "A")
    run_F2 = stop_after is None

    es = ExitStack()
    with es:
        E = es.enter_context

        def replay_all():
            for s in P.sems:
                s.handle = E(nc.semaphore(s.name))
            with nc.Block() as block:
                def mk(name):
                    def body(eng):
                        seen = {}
                        for fn, waits, inc, amt in P.stream[name]:
                            for s, v in waits:
                                if seen.get(s.name, -1) < v:
                                    eng.wait_ge(s.handle, v)
                                    seen[s.name] = v
                            if fn is None:
                                continue
                            inst = fn(eng)
                            if inc is not None:
                                inst.then_inc(inc.handle, amt)
                    return body
                block.tensor(mk("tensor"))
                block.vector(mk("vector"))
                block.scalar(mk("scalar"))
                block.sync(mk("sync"))
                block.gpsimd(mk("gpsimd"))
        uber0 = E(nc.sbuf_tensor("uber0", [128, 32768], BF16))
        uber1 = E(nc.sbuf_tensor("uber1", [128, 32768], BF16))
        ident = E(nc.sbuf_tensor("ident_sb", [128, 128], BF16))
        AR = E(nc.sbuf_tensor("arena", [128, ARENA_B], U8))
        consts = E(nc.sbuf_tensor("consts", [128, 4], F32))
        banks = [E(nc.psum_tensor(f"pb{i}", [128, 512], F32)) for i in range(8)]
        sD, sG, sT, sV, sS = (P.sem(n) for n in ("sD", "sG", "sT", "sV", "sS"))

        def bankbf(i):
            return banks[i][:, 0:512].bitcast(BF16)

        def arf(off, cols, p0=0, p1=128):
            return AR[p0:p1, off:off + 4 * cols].bitcast(F32)

        def arb(off, cols, p0=0, p1=128):
            return AR[p0:p1, off:off + 2 * cols].bitcast(BF16)

        F_XG = (0, 16384)
        F_XNT = 32768
        F_XN = 40960
        F_RT = (49152, 50176)
        F_STATS = 51200
        F_R2 = 51328

        P_WL = (0, 16384)
        P_NAT = (32768, 40960)
        P_SGN = 49152
        P_ABS = 53248
        P_STT = 57344
        P_FBB = 61440
        P_GT = 61504
        P_ST = 61568
        P_XSB = 61952
        P_UTB = 70144
        P_VBF = 78336

        AB_XGA = (0, 16384)
        AB_XN = (32768, 34816)
        AB_EKQ = 36864
        AB_VAUG = 38912
        AB_RK = 41088
        AB_BC = 41216
        AB_QH = 43264
        AB_AD = 45312
        AB_ATT = 45440
        AB_ATT2 = 47488
        AB_XO = (49536, 53632)
        AB_CTX = 57728
        AB_EQT = 59904
        AB_BROW = 64000
        AB_ONES = 72192
        AB_ST = 72448

        def LN(src, dst_bf, st_off, waits_v=(), waits_s=(), waits_apply=()):
            st = arf(st_off, 8)
            t0 = P.op("vector", lambda v, src=src, st=st: v.tensor_reduce(
                out=st[:, 0:1], in_=src, axis=AX.X, op=ALU.add),
                waits=waits_v, inc=sV)
            t1 = P.op("scalar", lambda s, src=src, st=st, dst=dst_bf:
                      s.activation(out=dst, in_=src, func=AF.Square,
                                   accum_out=st[:, 1:2]),
                      waits=tuple(waits_s) + tuple(waits_apply), inc=sS)
            tm = P.op("vector", lambda v, st=st: v.tensor_scalar_mul(
                out=st[:, 2:3], in0=st[:, 0:1], scalar1=1.0 / C),
                waits=[(sV, t0)], inc=sV)
            tm2 = P.op("vector", lambda v, st=st: v.tensor_tensor(
                out=st[:, 3:4], in0=st[:, 2:3], in1=st[:, 2:3], op=ALU.mult),
                waits=[(sV, tm)], inc=sV)
            t2 = P.op("vector", lambda v, st=st: v.tensor_scalar(
                out=st[:, 4:5], in0=st[:, 1:2], scalar1=1.0 / C,
                scalar2=st[:, 3:4], op0=ALU.mult, op1=ALU.subtract),
                waits=[(sS, t1), (sV, tm2)], inc=sV)
            tln = P.op("scalar", lambda s, st=st: s.activation(
                out=st[:, 5:6], in_=st[:, 4:5], func=AF.Ln,
                bias=consts[:, 0:1]), waits=[(sV, t2)], inc=sS)
            t3 = P.op("scalar", lambda s, st=st: s.activation(
                out=st[:, 6:7], in_=st[:, 5:6], func=AF.Exp, scale=-0.5),
                waits=[(sS, tln)], inc=sS)
            return P.op("vector", lambda v, src=src, dst=dst_bf, st=st:
                        v.tensor_scalar(out=dst, in0=src, scalar1=st[:, 2:3],
                                        scalar2=st[:, 6:7], op0=ALU.subtract,
                                        op1=ALU.mult),
                        waits=list(waits_apply) + [(sS, t3)], inc=sV)

        def T8(src_bf, bank_i, waits=()):
            pb = bankbf(bank_i)
            tok = None
            for cc in range(8):
                tok = P.op("tensor", lambda t, cc=cc, src=src_bf, pb=pb:
                           t.transpose(out=pb[:, cc * 128:(cc + 1) * 128],
                                       in_=src[:, cc * 128:(cc + 1) * 128],
                                       identity=ident[:]),
                           waits=(waits if cc == 0 else ()),
                           inc=(sT if cc == 7 else None))
            return tok

        fb_bf = arb(P_FBB, 16, 0, 16)
        gt = arf(P_GT, 1, 0, 16)
        uT_bf = arb(P_UTB, HID, 0, 16)
        v_bf = arb(P_VBF, C, 0, 16)
        X_sb = arb(P_XSB, HID, 0, 16)
        pst = arf(P_ST, 32)

        P.op("vector", lambda v: v.memset(consts[:, 0:1], 1e-5))
        t_id = P.dma("sync", lambda s: s.dma_start(out=ident[:], in_=ident_in),
                     inc=sD)
        t_gt = P.dma("sync", lambda s: s.dma_start(out=gt, in_=gate_in), inc=sD)
        t_fb = P.dma("gpsimd", lambda g: g.dma_start(out=fb_bf, in_=fb_in),
                     inc=sG)
        t_ut = P.dma("gpsimd", lambda g: g.dma_start(out=uT_bf, in_=uT_in),
                     inc=sG)
        t_vl = P.dma("gpsimd", lambda g: g.dma_start(out=v_bf, in_=v_in),
                     inc=sG)

        xev = 0
        for jb in range(8):
            w = [(sG, t_fb), (sG, t_ut)] if jb == 0 else [(sV, xev)]
            tk = P.op("tensor", lambda t, jb=jb: t.matmul(
                out=banks[7][0:16, 0:512], lhsT=fb_bf,
                rhs=uT_bf[:, jb * 512:(jb + 1) * 512], start=True, stop=True),
                waits=w, inc=sT)
            xev = P.op("vector", lambda v, jb=jb: v.tensor_scalar_mul(
                out=X_sb[:, jb * 512:(jb + 1) * 512],
                in0=banks[7][0:16, 0:512], scalar1=gt),
                waits=[(sT, tk), (sD, t_gt)], inc=sV)

        wl = [arf(P_WL[i], HID) for i in range(2)]
        nat = [arb(P_NAT[i], HID) for i in range(2)]
        sgn = arf(P_SGN, C)
        ab = arf(P_ABS, C)
        sttb = arf(P_STT, C)
        sttb2 = arf(P_NAT[0] + 2048, C)

        cons = [dict(v=0, s=0) for _ in range(2)]
        evac_tok = 0
        last_stt = 0
        tmask = 0
        ttk_prev = 0
        u1v = uber1.rearrange("p (k j c) -> p k j c", k=4, j=8)
        for dt in range(8):
            t_wl = P.dma("sync", lambda s, dt=dt: s.dma_start(
                out=wl[0], in_=w_proj[dt * 128:(dt + 1) * 128, :]),
                waits=[(sV, cons[0]["v"]), (sS, cons[0]["s"])], inc=sD)
            t_abs = P.op("scalar", lambda s: s.activation(
                out=wl[1], in_=wl[0], func=AF.Abs),
                waits=[(sD, t_wl), (sV, cons[0]["v"])], inc=sS)
            t0 = P.op("vector", lambda v: v.tensor_reduce(
                out=pst[:, 0:1], in_=wl[1], axis=AX.X, op=ALU.add),
                waits=[(sS, t_abs)], inc=sV)
            tm0 = P.op("vector", lambda v: v.tensor_scalar_mul(
                out=pst[:, 1:2], in0=pst[:, 0:1], scalar1=1.0 / HID),
                waits=[(sV, t0)], inc=sV)
            teq = 0
            for q in range(4):
                qs = slice(q * C, (q + 1) * C)
                P.op("vector", lambda v, qs=qs: v.tensor_scalar(
                    out=sttb, in0=wl[1][:, qs], scalar1=pst[:, 1:2],
                    scalar2=1.0, op0=ALU.subtract, op1=ALU.mult),
                    waits=[(sV, tm0)] if q == 0 else [])
                teq = P.op("vector", lambda v, q=q: v.tensor_reduce(
                    out=pst[:, 3 + q:4 + q], in_=sttb, axis=AX.X, op=ALU.add),
                    inc=sV)
            tE = P.op("vector", lambda v: v.tensor_reduce(
                out=pst[:, 7:8], in_=pst[:, 3:7], axis=AX.X, op=ALU.add),
                waits=[(sV, teq)], inc=sV)
            tEp = P.op("vector", lambda v: v.tensor_scalar_mul(
                out=pst[:, 2:3], in0=pst[:, 7:8], scalar1=1.0 / HID),
                waits=[(sV, tE)], inc=sV)
            tv1 = P.op("vector", lambda v: v.tensor_scalar(
                out=pst[:, 7:8], in0=pst[:, 1:2], scalar1=pst[:, 2:3],
                scalar2=consts[:, 0:1], op0=ALU.add, op1=ALU.max),
                waits=[(sV, tEp)], inc=sV)
            tsc = P.op("vector", lambda v: v.tensor_scalar_mul(
                out=pst[:, 7:8], in0=pst[:, 7:8], scalar1=0.5),
                waits=[(sV, tv1)], inc=sV)
            t_s = None
            for q in range(4):
                qs = slice(q * C, (q + 1) * C)
                P.op("vector", lambda v, qs=qs: v.tensor_scalar(
                    out=ab, in0=wl[1][:, qs], scalar1=2.0,
                    scalar2=pst[:, 1:2], op0=ALU.mult, op1=ALU.subtract),
                    waits=[(sV, tsc)] if q == 0 else [])
                tmask = P.op("vector", lambda v: v.tensor_scalar(
                    out=ab, in0=ab, scalar1=pst[:, 2:3], scalar2=1.0,
                    op0=ALU.is_gt, op1=ALU.mult), inc=sV)
                t_s = P.op("scalar", lambda s, qs=qs: s.activation(
                    out=sgn, in_=wl[0][:, qs], func=AF.Sign),
                    waits=[(sV, last_stt)] if last_stt else [], inc=sS)
                wst = [(sS, t_s), (sV, tmask)]
                if q == 0:
                    wst.append((sT, ttk_prev))
                last_stt = P.op("vector", lambda v, qs=qs:
                                v.scalar_tensor_tensor(
                                    out=nat[0][:, qs], in0=sgn,
                                    scalar=pst[:, 7:8], in1=ab,
                                    op0=ALU.mult, op1=ALU.mult),
                                waits=wst, inc=sV)
            cons[0] = dict(v=last_stt, s=t_s)
            ttk = None
            for jc in range(32):
                w = ([(sV, last_stt), (sS, evac_tok), (sV, xev), (sD, t_id)]
                     if jc == 0 else [])
                ttk = P.op("tensor", lambda t, jc=jc: t.transpose(
                    out=bankbf(jc % 8)[:, (jc // 8) * 128:(jc // 8 + 1) * 128],
                    in_=nat[0][:, jc * 128:(jc + 1) * 128], identity=ident[:]),
                    waits=w, inc=(sT if jc == 31 else None))
            ttk_prev = ttk
            for bb in range(8):
                evac_tok = P.op("scalar", lambda s, bb=bb, dt=dt: s.copy(
                    out=u1v[:, :, bb, dt * 128:(dt + 1) * 128],
                    in_=bankbf(bb)[:, 0:512].rearrange("p (k n) -> p k n", k=4)),
                    waits=([(sT, ttk)] if bb == 0 else []), inc=sS)
        wproj_evac_last = evac_tok

        fc_evac = 0
        lr_tok = 0
        ttk_pp = [ttk_prev, ttk_prev]
        sbufs = [sgn, ab, sttb]
        for jt0 in range(0, 32, 4):
            wsrc = w_fc[jt0 * 128:(jt0 + 4) * 128, :].rearrange(
                "(s p) c -> p s c", p=128)
            t_wl_fc = P.dma(
                "sync", lambda s, wsrc=wsrc:
                s.dma_start(out=wl[0].rearrange("p (s c) -> p s c", c=C),
                            in_=wsrc),
                waits=[(sV, cons[0]["v"]), (sS, cons[0]["s"])], inc=sD)
            t_absf = P.op("scalar", lambda s: s.activation(
                out=wl[1], in_=wl[0], func=AF.Abs),
                waits=[(sD, t_wl_fc), (sV, cons[0]["v"])], inc=sS)
            tred = None
            for q in range(4):
                tred = P.op("vector", lambda v, q=q: v.tensor_reduce(
                    out=pst[:, q:q + 1], in_=wl[1][:, q * C:(q + 1) * C],
                    axis=AX.X, op=ALU.add),
                    waits=[(sS, t_absf)] if q == 0 else [],
                    inc=(sV if q == 3 else None))
            tm = P.op("vector", lambda v: v.tensor_scalar_mul(
                out=pst[:, 4:8], in0=pst[:, 0:4], scalar1=-1.0 / C),
                waits=[(sV, tred)], inc=sV)
            tE = [0] * 4
            for q in range(4):
                tds = P.op("scalar", lambda s, q=q: s.activation(
                    out=sbufs[q % 3], in_=wl[1][:, q * C:(q + 1) * C],
                    func=AF.Identity, bias=pst[:, 4 + q:5 + q]),
                    waits=([(sV, tm)] if q == 0 else
                           ([(sV, tE[0])] if q == 3 else [])), inc=sS)
                tE[q] = P.op("vector", lambda v, q=q: v.tensor_reduce(
                    out=pst[:, 8 + q:9 + q], in_=sbufs[q % 3],
                    axis=AX.X, op=ALU.add), waits=[(sS, tds)], inc=sV)
            tEp = P.op("vector", lambda v: v.tensor_scalar_mul(
                out=pst[:, 12:16], in0=pst[:, 8:12], scalar1=1.0 / C),
                waits=[(sV, tE[3])], inc=sV)
            tsv = P.op("vector", lambda v: v.tensor_tensor(
                out=pst[:, 16:20], in0=pst[:, 12:16], in1=pst[:, 4:8],
                op=ALU.subtract), waits=[(sV, tEp)], inc=sV)
            tcl = P.op("vector", lambda v: v.tensor_scalar(
                out=pst[:, 16:20], in0=pst[:, 16:20], scalar1=1.0,
                scalar2=consts[:, 0:1], op0=ALU.mult, op1=ALU.max),
                waits=[(sV, tsv)], inc=sV)
            t_stt = 0
            for q in range(4):
                jt = jt0 + q
                wlv = wl[0][:, q * C:(q + 1) * C]
                wla = wl[1][:, q * C:(q + 1) * C]
                dbuf = sbufs[q % 2]
                gbuf = sbufs[2]
                td2 = P.op("scalar", lambda s, wla=wla, dbuf=dbuf, q=q:
                           s.activation(out=dbuf, in_=wla, func=AF.Identity,
                                        scale=2.0, bias=pst[:, 4 + q:5 + q]),
                           waits=([(sV, tcl)] if q == 0 else
                                  [(sV, t_stt)]), inc=sS)
                t_sg = P.op("scalar", lambda s, wlv=wlv, gbuf=gbuf:
                            s.activation(out=gbuf, in_=wlv, func=AF.Sign),
                            waits=[(sV, t_stt)] if q > 0 else [], inc=sS)
                tmask = P.op("vector", lambda v, dbuf=dbuf, q=q:
                             v.tensor_scalar(
                                 out=dbuf, in0=dbuf,
                                 scalar1=pst[:, 12 + q:13 + q], scalar2=1.0,
                                 op0=ALU.is_gt, op1=ALU.mult),
                             waits=[(sS, td2)], inc=sV)
                wstt = [(sS, t_sg), (sV, tmask)]
                if jt == 0:
                    wstt.append((sT, ttk_prev))
                t_stt = P.op("vector", lambda v, dbuf=dbuf, gbuf=gbuf, q=q:
                             v.scalar_tensor_tensor(
                                 out=sttb2, in0=gbuf,
                                 scalar=pst[:, 16 + q:17 + q], in1=dbuf,
                                 op0=ALU.mult, op1=ALU.mult),
                             waits=wstt, inc=sV)
                cons[0] = dict(v=t_stt, s=t_sg)
                w0 = ([(sV, xev), (sG, t_vl), (sS, wproj_evac_last)]
                      if jt == 0 else [(sV, lr_tok)])
                P.op("tensor", lambda t, jt=jt: t.matmul(
                    out=banks[6][:, 0:512],
                    lhsT=X_sb[:, jt * 128:(jt + 1) * 128],
                    rhs=v_bf[:, 0:512], start=True, stop=True), waits=w0)
                tlr = P.op("tensor", lambda t, jt=jt: t.matmul(
                    out=banks[7][:, 0:512],
                    lhsT=X_sb[:, jt * 128:(jt + 1) * 128],
                    rhs=v_bf[:, 512:1024], start=True, stop=True), inc=sT)
                natv = nat[jt % 2][:, 0:C]
                P.op("vector", lambda v, natv=natv: v.tensor_tensor(
                    out=natv[:, 0:512], in0=sttb2[:, 0:512],
                    in1=banks[6][:, 0:512], op=ALU.add),
                    waits=[(sV, t_stt), (sT, tlr), (sT, ttk_pp[jt % 2])])
                lr_tok = P.op("vector", lambda v, natv=natv: v.tensor_tensor(
                    out=natv[:, 512:1024], in0=sttb2[:, 512:1024],
                    in1=banks[7][:, 0:512], op=ALU.add), inc=sV)
                ttk = None
                for cc in range(8):
                    if cc < 6:
                        dst = bankbf(cc)[:, (jt % 4) * 128:(jt % 4 + 1) * 128]
                    else:
                        dst = bankbf(cc - 6)[:, 512 + (jt % 4) * 128:
                                             512 + (jt % 4 + 1) * 128]
                    w = []
                    if cc == 0:
                        w = [(sV, lr_tok)]
                        if jt == 0:
                            w.append((sS, wproj_evac_last))
                        if jt % 4 == 0:
                            w.append((sS, fc_evac))
                    ttk = P.op("tensor", lambda t, dst=dst, cc=cc, natv=natv:
                               t.transpose(out=dst,
                                           in_=natv[:, cc * 128:(cc + 1) * 128],
                                           identity=ident[:]),
                               waits=w, inc=(sT if cc == 7 else None))
                ttk_pp[jt % 2] = ttk
            for cc in range(8):
                src2 = (bankbf(cc)[:, 0:512] if cc < 6
                        else bankbf(cc - 6)[:, 512:1024])
                fc_evac = P.op("scalar",
                               lambda s, cc=cc, jt0=jt0, src2=src2:
                               s.copy(out=uber0[:, cc * 4096 + jt0 * 128:
                                                cc * 4096 + jt0 * 128 + 512],
                                      in_=src2),
                               waits=([(sT, ttk)] if cc == 0 else []),
                               inc=sS)
        P.barrier()

        t_sp0 = P.dma("gpsimd", lambda g: g.dma_start(out=wfc_scr, in_=uber0[:]),
                      waits=[(sS, fc_evac)], inc=sG)
        t_sp1 = P.dma("gpsimd", lambda g: g.dma_start(out=wpr_scr, in_=uber1[:]),
                      waits=[(sS, wproj_evac_last)], inc=sG)

        early_reload = {}

        def phase_F(xin, xout, reload):
            xg = [arf(F_XG[i], HID) for i in range(2)]
            xn = [arb(F_XN + s4 * 2048, C) for s4 in range(4)]
            xntv = AR[:, F_XNT:F_XNT + 2 * HID].bitcast(BF16).rearrange(
                "p (cc n) -> p cc n", n=512)
            rt = [arb(F_RT[i], 512) for i in range(2)]
            r2 = arb(F_R2, 8192)
            re0w = re1w = (sD, 0)
            if reload:
                if early_reload:
                    re0w = early_reload["re0"]
                    re1w = early_reload["re1"]
                else:
                    re0w = (sD, P.dma("sync", lambda s: s.dma_start(
                        out=uber0[:], in_=wfc_scr), inc=sD))
                    re1w = (sD, P.dma("sync", lambda s: s.dma_start(
                        out=uber1[:], in_=wpr_scr), inc=sD))
            store_tok = [0] * 8
            t_xl = [0] * 8
            ln_tok = {}
            t8_toks = {}
            mm1_last = 0
            mm2_last = 0
            ev2_last = 0
            ev2_tok = [0, 0]
            relu_tok = [0] * 32
            sq_tok = [0] * 32
            xnt_evac = [0, 0, 0, 0]

            def emit_load(g):
                src = xin[g * 512:(g + 1) * 512, :].rearrange(
                    "(s p) c -> p s c", p=128)
                t_xl[g] = P.dma("sync", lambda s, src=src, g=g: s.dma_start(
                    out=xg[g % 2].rearrange("p (s c) -> p s c", c=C), in_=src),
                    waits=([(sD, store_tok[g - 2])] if g >= 2 else []),
                    inc=sD)

            def emit_LN_group(g):
                for s4 in range(4):
                    first = [(sD, t_xl[g])] if s4 == 0 else []
                    ln_tok[(g, s4)] = LN(
                        xg[g % 2][:, s4 * C:(s4 + 1) * C], xn[s4], F_STATS,
                        waits_v=first, waits_s=first,
                        waits_apply=[(sT, t8_toks.get((g - 1, s4), 0))])

            emit_load(0)
            emit_load(1)
            emit_LN_group(0)
            for g in range(8):
                bg = g % 2
                for s4 in range(4):
                    w = [(sV, ln_tok[(g, s4)])]
                    if s4 >= 2:
                        w.append((sS, xnt_evac[s4 - 2]))
                    elif g > 0:
                        w.append((sS, xnt_evac[s4 + 2]))
                    t8_toks[(g, s4)] = T8(xn[s4], 6 + s4 % 2, w)
                    ew = [(sT, t8_toks[(g, s4)])]
                    if s4 == 0:
                        ew.append((sT, mm1_last))
                    xnt_evac[s4] = P.op(
                        "scalar", lambda s, s4=s4: s.copy(
                            out=xntv[:, :, s4 * 128:(s4 + 1) * 128],
                            in_=bankbf(6 + s4 % 2).rearrange(
                                "p (cc n) -> p cc n", n=128)),
                        waits=ew, inc=sS)
                for half in range(2):
                    for jc in range(half * 16, half * 16 + 16):
                        tok = None
                        for cc in range(8):
                            w = []
                            if cc == 0:
                                if jc < 2:
                                    w.append((sS, xnt_evac[3]))
                                else:
                                    w.append((sS, relu_tok[jc - 2]))
                                if g == 0 and jc == 0 and reload:
                                    w.append(re0w)
                            tok = P.op(
                                "tensor", lambda t, jc=jc, cc=cc: t.matmul(
                                    out=banks[jc % 2][:, 0:512],
                                    lhsT=uber0[:, cc * 4096 + jc * 128:
                                               cc * 4096 + (jc + 1) * 128],
                                    rhs=xntv[:, cc, :],
                                    start=(cc == 0), stop=(cc == 7)),
                                waits=w, inc=(sT if cc == 7 else None))
                        mm1_last = tok
                        relu_tok[jc] = P.op(
                            "scalar", lambda s, jc=jc: s.activation(
                                out=rt[jc % 2][:], in_=banks[jc % 2][:, 0:512],
                                func=AF.Relu),
                            waits=[(sT, tok)] + (
                                [(sV, sq_tok[jc - 2])] if jc >= 2 else []),
                            inc=sS)
                        wsq = [(sS, relu_tok[jc])]
                        if jc % 16 == 0:
                            wsq.append((sT, mm2_last))
                        sq_tok[jc] = P.op(
                            "vector", lambda v, jc=jc: v.tensor_tensor(
                                out=r2[:, (jc % 16) * 512:
                                       (jc % 16 + 1) * 512],
                                in0=rt[jc % 2][:], in1=rt[jc % 2][:],
                                op=ALU.mult),
                            waits=wsq, inc=sV)
                    if half == 0 and g + 1 < 8:
                        emit_LN_group(g + 1)
                    for era in range(4):
                        dh, nsp = era // 2, era % 2
                        bp = 2 + 2 * (era % 2)
                        tok = None
                        for jc in range(half * 16, half * 16 + 16):
                            for ns2 in range(2):
                                ns = 2 * nsp + ns2
                                w = []
                                if ns2 == 0:
                                    w.append((sV, sq_tok[jc]))
                                    if jc % 16 == 0:
                                        w.append((sV, ev2_tok[era % 2]))
                                        if g == 0 and half == 0 and era == 0 \
                                                and reload:
                                            w.append(re1w)
                                tok = P.op(
                                    "tensor",
                                    lambda t, dh=dh, jc=jc, ns=ns, bp=bp,
                                    ns2=ns2: t.matmul(
                                        out=banks[bp + ns2][:, 0:512],
                                        lhsT=r2[:, (jc % 16) * 512 + ns * 128:
                                                (jc % 16) * 512 + (ns + 1) * 128],
                                        rhs=uber1[:, jc * 1024 + dh * 512:
                                                  jc * 1024 + (dh + 1) * 512],
                                        start=(jc % 16 == 0),
                                        stop=(jc % 16 == 15)),
                                    waits=w,
                                    inc=(sT if (jc % 16 == 15 and ns2 == 1)
                                         else None))
                        mm2_last = tok
                        for ns2 in range(2):
                            ns = 2 * nsp + ns2
                            ev2_tok[era % 2] = P.op(
                                "vector", lambda v, dh=dh, ns=ns, bp=bp,
                                ns2=ns2, bg=bg: v.tensor_tensor(
                                    out=xg[bg][:, ns * 1024 + dh * 512:
                                               ns * 1024 + (dh + 1) * 512],
                                    in0=xg[bg][:, ns * 1024 + dh * 512:
                                               ns * 1024 + (dh + 1) * 512],
                                    in1=banks[bp + ns2][:, 0:512], op=ALU.add),
                                waits=([(sT, mm2_last)] if ns2 == 0 else []),
                                inc=sV)
                        ev2_last = ev2_tok[era % 2]
                dst = xout[g * 512:(g + 1) * 512, :].rearrange(
                    "(s p) c -> p s c", p=128)
                store_tok[g] = P.dma(
                    "sync", lambda s, dst=dst, bg=bg: s.dma_start(
                        out=dst,
                        in_=xg[bg].rearrange("p (s c) -> p s c", c=C)),
                    waits=[(sV, ev2_last)], inc=sD)
                if g + 2 < 8:
                    emit_load(g + 2)
            return store_tok

        if run_F1:
            phase_F(x_in, x1_hbm, reload=False)
        P.barrier()

        if run_A:
            xga = [arf(AB_XGA[i], HID) for i in range(2)]
            xnab = [arb(AB_XN[i], C) for i in range(2)]
            ekq = arb(AB_EKQ, C)
            ekq3 = AR[:, AB_EKQ:AB_EKQ + 2 * C].bitcast(BF16).rearrange(
                "p (h e) -> p h e", e=64)
            vaug = arb(AB_VAUG, 1056)
            vaug3 = AR[:, AB_VAUG:AB_VAUG + 2 * 1056].bitcast(BF16).rearrange(
                "p (h e) -> p h e", e=66)
            rk = arf(AB_RK, 32)
            bc = arb(AB_BC, C)
            bc3 = AR[:, AB_BC:AB_BC + 2 * C].bitcast(BF16).rearrange(
                "p (h e) -> p h e", e=64)
            qh = arb(AB_QH, C)
            aD = arf(AB_AD, 16)
            att = arb(AB_ATT, C)
            att3 = AR[:, AB_ATT:AB_ATT + 2 * C].bitcast(BF16).rearrange(
                "p (h e) -> p h e", e=64)
            attT = arb(AB_ATT2, C)
            xo = [arf(AB_XO[i], C) for i in range(2)]
            ctx = arb(AB_CTX, 1056, 0, 64)
            eqT = arb(AB_EQT, 2048, 0, 64)
            eqT3 = AR[0:64, AB_EQT:AB_EQT + 2 * 2048].bitcast(BF16).rearrange(
                "p (h2 two n) -> p h2 two n", two=2, n=128)
            brow = arb(AB_BROW, 4 * C, 0, 1)
            ones = arb(AB_ONES, 128, 0, 1)
            u0v = uber0.rearrange("p (cc n) -> p cc n", n=4096)

            t_ones = P.op("vector", lambda v: v.memset(ones, 1.0), inc=sV)
            t_brow = P.dma("gpsimd", lambda g: g.dma_start(
                out=brow, in_=brow_in), inc=sG)
            t_w = [0] * 4
            for widx in (1, 2, 0, 3):
                for cc in range(8):
                    t_w[widx] = P.dma(
                        "gpsimd", lambda g, widx=widx, cc=cc: g.dma_start(
                            out=uber1[:, widx * 8192 + cc * 1024:
                                      widx * 8192 + (cc + 1) * 1024],
                            in_=wT_in[widx][cc * 128:(cc + 1) * 128, :]),
                        inc=sG)

            def emit_proj(widx, bank0, tc, extra_w=()):
                tok = None
                for jh in range(2):
                    for cc in range(8):
                        w = list(extra_w) if (jh == 0 and cc == 0) else []
                        P.op("tensor", lambda t, widx=widx, jh=jh, cc=cc,
                             tc=tc, bank0=bank0: t.matmul(
                                 out=banks[bank0 + jh][:, 0:512],
                                 lhsT=u0v[:, cc, tc * 128:(tc + 1) * 128],
                                 rhs=uber1[:,
                                           widx * 8192 + cc * 1024 + jh * 512:
                                           widx * 8192 + cc * 1024
                                           + (jh + 1) * 512],
                                 start=(cc == 0), stop=False), waits=w)
                    tok = P.op("tensor",
                               lambda t, widx=widx, jh=jh, bank0=bank0:
                               t.matmul(out=banks[bank0 + jh][:, 0:512],
                                        lhsT=ones,
                                        rhs=brow[0:1, widx * C + jh * 512:
                                                 widx * C + (jh + 1) * 512],
                                        start=False, stop=True), inc=sT)
                return tok

            ln_a = [0] * 32
            t_xga = [0] * 8
            t8a = [0] * 32
            xnt_ev_a = [0] * 32
            kmm_tok = [0] * 32
            ek_tok = 0
            ctx_tok = 0
            vaug_tok = 0

            def emit_load_A(ga, cons_tok, toks):
                src = x1_hbm[ga * 512:(ga + 1) * 512, :].rearrange(
                    "(s p) c -> p s c", p=128)
                toks[ga] = P.dma("sync", lambda s, src=src, ga=ga: s.dma_start(
                    out=xga[ga % 2].rearrange("p (s c) -> p s c", c=C),
                    in_=src),
                    waits=([(sV, cons_tok)] if cons_tok else []), inc=sD)

            def emit_LN_A(tc, t8prev):
                first = [(sD, t_xga[tc // 4])] if tc % 4 == 0 else []
                ln_a[tc] = LN(
                    xga[(tc // 4) % 2][:, (tc % 4) * C:(tc % 4 + 1) * C],
                    xnab[tc % 2], AB_ST,
                    waits_v=first, waits_s=first,
                    waits_apply=[(sT, t8prev)])

            emit_load_A(0, 0, t_xga)
            emit_LN_A(0, 0)
            t8a[0] = T8(xnab[0], 7, [(sV, ln_a[0])])
            xnt_ev_a[0] = P.op("scalar", lambda s: s.copy(
                out=u0v[:, :, 0:128],
                in_=bankbf(7).rearrange("p (cc n) -> p cc n", n=128)),
                waits=[(sT, t8a[0])], inc=sS)
            kmm_tok[0] = emit_proj(1, 0, 0,
                                   extra_w=[(sG, t_w[1]), (sV, t_ones),
                                            (sG, t_brow), (sS, xnt_ev_a[0])])
            for tc in range(32):
                if tc + 1 < 32:
                    if (tc + 1) % 4 == 0:
                        ga = (tc + 1) // 4
                        emit_load_A(ga,
                                    ln_a[4 * (ga - 2) + 3] if ga >= 2 else 0,
                                    t_xga)
                    emit_LN_A(tc + 1, t8a[tc])
                w = [(sT, kmm_tok[tc])]
                if tc >= 1:
                    w.append((sT, ctx_tok))
                for jh in range(2):
                    ek_tok = P.op("scalar", lambda s, jh=jh: s.activation(
                        out=ekq[:, jh * 512:(jh + 1) * 512],
                        in_=banks[jh][:, 0:512], func=AF.Exp),
                        waits=(w if jh == 0 else []), inc=sS)
                tsk = P.op("vector", lambda v: v.tensor_reduce(
                    out=rk[:, 0:16], in_=ekq3, axis=AX.X, op=ALU.add),
                    waits=[(sS, ek_tok)], inc=sV)
                trk = P.op("vector", lambda v: v.reciprocal(
                    out=rk[:, 16:32], in_=rk[:, 0:16]),
                    waits=[(sV, tsk)], inc=sV)
                tbc = P.op("vector", lambda v: v.tensor_copy(
                    out=bc3, in_=rk[:, 16:32].unsqueeze(2).broadcast_to(
                        [128, 16, 64])), waits=[(sV, trk)], inc=sV)
                vmm = emit_proj(2, 2, tc,
                                extra_w=([(sG, t_w[2])] if tc == 0 else
                                         [(sV, vaug_tok)]))
                if tc + 1 < 32:
                    t8a[tc + 1] = T8(xnab[(tc + 1) % 2], 7,
                                     [(sV, ln_a[tc + 1]), (sS, xnt_ev_a[tc])])
                    xnt_ev_a[tc + 1] = P.op(
                        "scalar", lambda s, tc=tc: s.copy(
                            out=u0v[:, :, (tc + 1) * 128:(tc + 2) * 128],
                            in_=bankbf(7).rearrange("p (cc n) -> p cc n",
                                                    n=128)),
                        waits=[(sT, t8a[tc + 1])], inc=sS)
                    kmm_tok[tc + 1] = emit_proj(
                        1, 0, tc + 1,
                        extra_w=[(sS, xnt_ev_a[tc + 1]), (sS, ek_tok)])
                w = [(sT, vmm), (sV, tbc)]
                if tc >= 1:
                    w.append((sT, ctx_tok))
                for b2 in range(2):
                    P.op("vector", lambda v, b2=b2: v.tensor_tensor(
                        out=vaug3[:, b2 * 8:(b2 + 1) * 8, 0:64],
                        in0=banks[2 + b2].rearrange("p (h e) -> p h e", e=64),
                        in1=bc3[:, b2 * 8:(b2 + 1) * 8, :], op=ALU.mult),
                        waits=(w if b2 == 0 else []))
                vaug_tok = P.op("vector", lambda v: v.tensor_copy(
                    out=vaug3[:, :, 64:65],
                    in_=rk[:, 16:32].unsqueeze(2)), inc=sV)
                for h in range(16):
                    w = [(sV, vaug_tok)] if h == 0 else []
                    ctx_tok = P.op("tensor", lambda t, h=h, tc=tc: t.matmul(
                        out=banks[4 + h // 7][0:64,
                                              (h % 7) * 66:(h % 7) * 66 + 65],
                        lhsT=ekq[:, h * 64:(h + 1) * 64],
                        rhs=vaug[:, h * 66:h * 66 + 65],
                        start=(tc == 0 and h % 7 == 0),
                        stop=(tc == 31 and (h % 7 == 6 or h == 15))),
                        waits=w, inc=(sT if h == 15 else None))
            ctx_ev = 0
            for b in range(3):
                nh = (7, 7, 2)[b]
                ctx_ev = P.op("scalar", lambda s, b=b, nh=nh: s.copy(
                    out=ctx[:, b * 462:b * 462 + nh * 66],
                    in_=banks[4 + b][0:64, 0:nh * 66]),
                    waits=([(sT, ctx_tok)] if b == 0 else []), inc=sS)
            if stop_after == "A":
                P.dma("sync", lambda s: s.dma_start(out=ctx_dbg, in_=ctx),
                      waits=[(sS, ctx_ev)], inc=sD)
                P.dma("sync", lambda s: s.dma_start(out=wfc_scr, in_=uber0[:]),
                      waits=[(sS, ctx_ev)], inc=sD)

            P.barrier()

            if not run_B:
                P.barrier()
                P.check()
                replay_all()
                return nc
            eq_tok = 0
            qh_tok = 0
            att1_last = 0
            attT_ev = 0
            o1_tok = 0
            x2ev = [0] * 32
            x2st = [0] * 32
            qmm_tok = [0] * 32
            t_xgb = [0] * 8

            def emit_load_B(ga):
                src = x1_hbm[ga * 512:(ga + 1) * 512, :].rearrange(
                    "(s p) c -> p s c", p=128)
                t_xgb[ga] = P.dma("sync", lambda s, src=src, ga=ga: s.dma_start(
                    out=xga[ga % 2].rearrange("p (s c) -> p s c", c=C),
                    in_=src),
                    waits=([(sV, x2ev[4 * (ga - 2) + 3])] if ga >= 2 else []),
                    inc=sD)

            emit_load_B(0)
            qmm_tok[0] = emit_proj(0, 0, 0, extra_w=[(sG, t_w[0])])
            t8q_prev = 0
            t8t_prev = 0
            attT_evs = [0] * 32
            proj_tok = [0] * 32

            def emit_projB(tc):
                pw = [(sS, attT_evs[tc])]
                if tc == 0:
                    pw.append((sG, t_w[3]))
                if tc >= 1:
                    pw.append((sV, x2ev[tc - 1]))
                ptok = None
                for jh in range(2):
                    for cc in range(8):
                        w = pw if (jh == 0 and cc == 0) else []
                        P.op("tensor", lambda t, jh=jh, cc=cc: t.matmul(
                            out=banks[5 + jh][:, 0:512],
                            lhsT=attT[:, cc * 128:(cc + 1) * 128],
                            rhs=uber1[:, 3 * 8192 + cc * 1024 + jh * 512:
                                      3 * 8192 + cc * 1024 + (jh + 1) * 512],
                            start=(cc == 0), stop=False), waits=w)
                    ptok = P.op("tensor", lambda t, jh=jh: t.matmul(
                        out=banks[5 + jh][:, 0:512], lhsT=ones,
                        rhs=brow[0:1, 3 * C + jh * 512:3 * C + (jh + 1) * 512],
                        start=False, stop=True), inc=sT)
                proj_tok[tc] = ptok

            def emit_x2(tc):
                for jh in range(2):
                    w = []
                    if jh == 0:
                        w = [(sT, proj_tok[tc]), (sD, t_xgb[tc // 4])]
                        if tc >= 2:
                            w.append((sD, x2st[tc - 2]))
                    x2ev[tc] = P.op(
                        "vector", lambda v, jh=jh, tc=tc: v.tensor_tensor(
                            out=xo[tc % 2][:, jh * 512:(jh + 1) * 512],
                            in0=banks[5 + jh][:, 0:512],
                            in1=xga[(tc // 4) % 2][:,
                                                   (tc % 4) * 1024 + jh * 512:
                                                   (tc % 4) * 1024
                                                   + (jh + 1) * 512],
                            op=ALU.add), waits=w, inc=sV)
                x2st[tc] = P.dma("sync", lambda s, tc=tc: s.dma_start(
                    out=x2_hbm[tc * 128:(tc + 1) * 128, :], in_=xo[tc % 2]),
                    waits=[(sV, x2ev[tc])], inc=sD)

            for tc in range(32):
                if (tc + 1) % 4 == 0 and tc + 1 < 32:
                    emit_load_B((tc + 1) // 4)
                w = [(sT, qmm_tok[tc])]
                if tc >= 1:
                    w.append((sV, qh_tok))
                for jh in range(2):
                    eq_tok = P.op("scalar", lambda s, jh=jh: s.activation(
                        out=ekq[:, jh * 512:(jh + 1) * 512],
                        in_=banks[jh][:, 0:512], func=AF.Exp),
                        waits=(w if jh == 0 else []), inc=sS)
                tsq = P.op("vector", lambda v: v.tensor_reduce(
                    out=rk[:, 0:16], in_=ekq3, axis=AX.X, op=ALU.add),
                    waits=[(sS, eq_tok)], inc=sV)
                trq = P.op("vector", lambda v: v.reciprocal(
                    out=rk[:, 16:32], in_=rk[:, 0:16]),
                    waits=[(sV, tsq)], inc=sV)
                tbc = P.op("vector", lambda v: v.tensor_copy(
                    out=bc3, in_=rk[:, 16:32].unsqueeze(2).broadcast_to(
                        [128, 16, 64])), waits=[(sV, trq)], inc=sV)
                qh_tok = P.op("vector", lambda v: v.tensor_tensor(
                    out=qh, in0=ekq, in1=bc, op=ALU.mult),
                    waits=[(sV, tbc)], inc=sV)
                if tc + 1 < 32:
                    qmm_tok[tc + 1] = emit_proj(0, 0, tc + 1,
                                                extra_w=[(sS, eq_tok)])
                w = [(sV, qh_tok)]
                if tc >= 1:
                    w.append((sS, attT_evs[tc - 1]))
                t8q = T8(qh, 7, w)
                t8q_prev = t8q
                pb7 = bankbf(7)
                ew = [(sT, t8q)]
                if tc >= 1:
                    ew.append((sT, o1_tok))
                P.op("scalar", lambda s: s.copy(
                    out=eqT3[:, :, 0, :],
                    in_=pb7[0:64, :].rearrange("p (cc n) -> p cc n", n=128)),
                    waits=ew)
                eqT_ev = P.op("scalar", lambda s: s.copy(
                    out=eqT3[:, :, 1, :],
                    in_=pb7[64:128, :].rearrange("p (cc n) -> p cc n", n=128)),
                    inc=sS)
                for h in range(16):
                    w = []
                    if h == 0:
                        w = [(sS, eqT_ev), (sS, ctx_ev)]
                        if tc >= 1:
                            w.append((sV, att1_last))
                    o1_tok = P.op("tensor", lambda t, h=h: t.matmul(
                        out=banks[2 + h // 7][:,
                                              (h % 7) * 66:(h % 7) * 66 + 65],
                        lhsT=eqT[:, h * 128:(h + 1) * 128],
                        rhs=ctx[:, h * 66:h * 66 + 65],
                        start=(h % 7 == 0),
                        stop=(h % 7 == 6 or h == 15)),
                        waits=w, inc=(sT if h == 15 else None))
                if tc >= 1:
                    emit_projB(tc - 1)
                tad = None
                for b in range(3):
                    nh = (7, 7, 2)[b]
                    tad = P.op("vector", lambda v, b=b, nh=nh: v.reciprocal(
                        out=aD[:, b * 7:b * 7 + nh],
                        in_=banks[2 + b][:, 0:nh * 66].rearrange(
                            "p (h e) -> p h e", e=66)[:, :, 64:65]),
                        waits=([(sT, o1_tok)] if b == 0 else []),
                        inc=(sV if b == 2 else None))
                tadb = P.op("vector", lambda v: v.tensor_copy(
                    out=bc3, in_=aD[:, 0:16].unsqueeze(2).broadcast_to(
                        [128, 16, 64])), waits=[(sV, tad)], inc=sV)
                for b in range(3):
                    nh = (7, 7, 2)[b]
                    w = [(sV, tadb)] if b == 0 else []
                    if b == 0 and tc >= 1:
                        w.append((sT, t8t_prev))
                    P.op("vector", lambda v, b=b, nh=nh: v.tensor_tensor(
                        out=att3[:, b * 7:b * 7 + nh, :],
                        in0=banks[2 + b][:, 0:nh * 66].rearrange(
                            "p (h e) -> p h e", e=66)[:, :, 0:64],
                        in1=bc3[:, b * 7:b * 7 + nh, :], op=ALU.mult),
                        waits=w)
                att1_last = P.op("vector", lambda v: v.tensor_tensor(
                    out=att, in0=att, in1=qh, op=ALU.add), inc=sV)
                t8t = T8(att, 7, [(sV, att1_last), (sS, eqT_ev)])
                t8t_prev = t8t
                aw = [(sT, t8t)]
                if tc >= 1:
                    aw.append((sT, proj_tok[tc - 1]))
                attT_evs[tc] = P.op("scalar", lambda s: s.copy(
                    out=attT, in_=bankbf(7)[:, 0:1024]),
                    waits=aw, inc=sS)
                if tc >= 1:
                    emit_x2(tc - 1)
            emit_projB(31)
            emit_x2(31)
            if run_F2:
                early_reload["re0"] = (sG, P.dma(
                    "gpsimd", lambda g: g.dma_start(out=uber0[:], in_=wfc_scr),
                    waits=[(sT, qmm_tok[31]), (sG, t_sp0)], inc=sG))
                early_reload["re1"] = (sG, P.dma(
                    "gpsimd", lambda g: g.dma_start(out=uber1[:], in_=wpr_scr),
                    waits=[(sT, proj_tok[31]), (sG, t_sp1)], inc=sG))

            P.barrier()

        if run_F2:
            phase_F(x2_hbm, out_hbm, reload=True)

        P.barrier()
        P.check()

        replay_all()
    return nc


def kernel(_stop_after=None, **inputs):
    f32 = np.float32
    x = np.asarray(inputs["L0_attn_x"], dtype=f32)

    def T(a):
        return np.ascontiguousarray(np.asarray(a, f32).T)

    shared = {
        "wqT": T(inputs["L0_attn_Wq"]),
        "wkT": T(inputs["L0_attn_Wk"]),
        "wvT": T(inputs["L0_attn_Wv"]),
        "wpT": T(inputs["L0_attn_Wp"]),
        "brow": np.ascontiguousarray(np.stack(
            [np.asarray(inputs["L0_attn_b" + k], f32) for k in "qkvp"]
        ).reshape(1, 4 * C)),
        "w_fc": np.ascontiguousarray(np.asarray(inputs["L0_ffn_w_fc"], f32)),
        "w_proj": np.ascontiguousarray(np.asarray(inputs["L0_ffn_w_proj"],
                                                  f32)),
        "fb": np.ascontiguousarray(np.asarray(inputs["L0_ffn_fast_b"], f32)),
        "uT": T(inputs["L0_ffn_u"]),
        "vlr": np.ascontiguousarray(np.asarray(inputs["L0_ffn_v"], f32)),
        "gate16": np.ascontiguousarray(np.broadcast_to(
            np.asarray(inputs["L0_ffn_gate"], f32).reshape(1, 1), (16, 1))),
        "ident": np.eye(128, dtype=ml_dtypes.bfloat16),
    }
    nc = build_program(_stop_after)
    in_maps = [dict(shared, x=np.ascontiguousarray(x[i])) for i in range(8)]
    res = run_bass_kernel_spmd(nc, in_maps, core_ids=list(range(8)))
    if _stop_after == "P":
        return (np.asarray(res.results[0]["wfcscr"]),
                np.asarray(res.results[0]["wprscr"]))
    if _stop_after == "A":
        return (np.asarray(res.results[0]["ctxdbg"]),
                np.asarray(res.results[0]["wfcscr"]),
                np.stack([np.asarray(r["x1s"]) for r in res.results]))
    key = {"F1": "x1s", "B": "x2s"}.get(_stop_after, "out")
    return np.stack([np.asarray(r[key]) for r in res.results])
